# version 20
# baseline (speedup 1.0000x reference)
import contextlib
import numpy as np
import ml_dtypes

import concourse.bass as bass
import concourse.mybir as mybir
from concourse.bass_utils import run_bass_kernel_spmd
from concourse.alu_op_type import AluOpType as OP

F32 = mybir.dt.float32
BF16 = mybir.dt.bfloat16
AF = mybir.ActivationFunctionType
AX = mybir.AxisListType
ts = bass.ts

BFnp = ml_dtypes.bfloat16

P = 128
T = 2048
TH = 2176
NTILE = TH // P
CH = 512
NCH = T // CH
SUB = 256
NSUB = CH // SUB
NET = 16
NN = 16
D = 1024
E = 2048
EPS = 1e-5
BETA = 0.6


class Sem:
    def __init__(self, nc, es, name):
        self.h = es.enter_context(nc.semaphore(name))
        self.v = 0

    def inc(self, instr, k=1):
        instr.then_inc(self.h, k)
        self.v += k
        return self.v

    def bump(self, k=1):
        self.v += k
        return self.v

    def wait(self, eng, v=None):
        eng.wait_ge(self.h, self.v if v is None else v)


def build():
    MK = {}

    def setmk(name, sem):
        MK[name] = (sem.h, sem.v)

    def waitmk(eng, name):
        if name in MK:
            h, v = MK[name]
            eng.wait_ge(h, v)

    nc = bass.Bass("TRN2", target_bir_lowering=False,
                   detect_race_conditions=False)
    ap = lambda t: t.ap()

    xh = ap(nc.dram_tensor("xh", [TH, D], F32, kind="ExternalInput"))
    inwT = ap(nc.dram_tensor("inwT", [2, D, 4096], F32, kind="ExternalInput"))
    xpwT = ap(nc.dram_tensor("xpwT", [2, E, 96], F32, kind="ExternalInput"))
    dtwT = ap(nc.dram_tensor("dtwT", [2, 64, E], F32, kind="ExternalInput"))
    outwT = ap(nc.dram_tensor("outwT", [2, E, D], F32, kind="ExternalInput"))
    cwp = ap(nc.dram_tensor("cwp", [P, 2, NET, 4], F32, kind="ExternalInput"))
    cbp = ap(nc.dram_tensor("cbp", [P, 2, NET], F32, kind="ExternalInput"))
    dtbp = ap(nc.dram_tensor("dtbp", [P, 2, NET], F32, kind="ExternalInput"))
    dpp = ap(nc.dram_tensor("dpp", [P, 2, NET], F32, kind="ExternalInput"))
    alogp = ap(nc.dram_tensor("alogp", [P, 2, NET, NN], F32, kind="ExternalInput"))
    nwp = ap(nc.dram_tensor("nwp", [P, 2, 8], F32, kind="ExternalInput"))
    gwp = ap(nc.dram_tensor("gwp", [P, 8, 16], F32, kind="ExternalInput"))
    gbt = ap(nc.dram_tensor("gbt", [P, 16], F32, kind="ExternalInput"))
    maskc = ap(nc.dram_tensor("maskc", [P, 1], F32, kind="ExternalInput"))
    idbf = ap(nc.dram_tensor("idbf", [P, P], BF16, kind="ExternalInput"))
    idf = ap(nc.dram_tensor("idf", [P, P], F32, kind="ExternalInput"))
    selp = ap(nc.dram_tensor("selp", [32, 32, P], BF16, kind="ExternalInput"))
    onesd = ap(nc.dram_tensor("onesd", [1, T], BF16, kind="ExternalInput"))
    swg = ap(nc.dram_tensor("swg", [D, 512], F32, kind="ExternalInput"))
    swu = ap(nc.dram_tensor("swu", [D, 512], F32, kind="ExternalInput"))
    swd = ap(nc.dram_tensor("swd", [512, D], F32, kind="ExternalInput"))
    ewg = ap(nc.dram_tensor("ewg", [16, D, 512], F32, kind="ExternalInput"))
    ewu = ap(nc.dram_tensor("ewu", [16, D, 512], F32, kind="ExternalInput"))
    ewd = ap(nc.dram_tensor("ewd", [16, 512, D], F32, kind="ExternalInput"))

    out = ap(nc.dram_tensor("out", [T, D], F32, kind="ExternalOutput"))

    dram = lambda n, shp, dt: ap(nc.dram_tensor(n, shp, dt, kind="Internal"))
    winw = dram("winw", [2, D, 4096], BF16)
    wxp = dram("wxp", [2, E, 96], BF16)
    wdt = dram("wdt", [2, 64, E], BF16)
    wout = dram("wout", [2, E, D], BF16)
    wswg = dram("wswg", [D, 512], BF16)
    wswu = dram("wswu", [D, 512], BF16)
    wswd = dram("wswd", [512, D], BF16)
    wewg = dram("wewg", [16, D, 512], BF16)
    wewu = dram("wewu", [16, D, 512], BF16)
    wewd = dram("wewd", [16, 512, D], BF16)
    u0fm = dram("u0fm", [D, TH], BF16)
    u1fm = dram("u1fm", [D, TH], BF16)
    xnfm = dram("xnfm", [D, TH], BF16)
    xnfm32 = dram("xnfm32", [D, TH], F32)
    xntm = dram("xntm", [TH, D], F32)
    r1tm = dram("r1tm", [TH, D], F32)
    mb1tm = dram("mb1tm", [TH, D], F32)
    ccin = [dram("cc1in", [P, 512], F32), dram("cc2in", [3, D], F32),
            dram("cc3in", [P, 512], F32)]
    ccout = [dram("cc1out", [P, 512], F32), dram("cc2out", [3, D], F32),
             dram("cc3out", [P, 512], F32)]
    PAIRS = [[0, 1], [2, 3], [4, 5], [6, 7]]

    es = contextlib.ExitStack()
    sb = lambda n, shp, dt: es.enter_context(nc.sbuf_tensor(n, shp, dt))
    pst = lambda n, shp, dt=F32: es.enter_context(nc.psum_tensor(n, shp, dt))

    SY, TE, VE, SE, GE = nc.sync, nc.tensor, nc.vector, nc.scalar, nc.gpsimd

    ST = Sem(nc, es, "ST")
    SV = Sem(nc, es, "SV")
    SS = Sem(nc, es, "SS")
    SST = Sem(nc, es, "SST")
    PK = Sem(nc, es, "PK")
    CCS = [Sem(nc, es, f"CC{i}") for i in (1, 2, 3)]
    RS = [Sem(nc, es, f"RS{i}") for i in (1, 2, 3)]
    GT = Sem(nc, es, "GT")
    SWA = Sem(nc, es, "SWA")
    SWIs = [Sem(nc, es, "SWI0"), Sem(nc, es, "SWI1")]
    SWU = Sem(nc, es, "SWU")
    SWT = Sem(nc, es, "SWT")
    SXP = Sem(nc, es, "SXP")
    SWX = Sem(nc, es, "SWX")
    WAs = [Sem(nc, es, "WA0"), Sem(nc, es, "WA1")]
    WOs = [Sem(nc, es, "WO0"), Sem(nc, es, "WO1")]
    WFs = [Sem(nc, es, "WF0"), Sem(nc, es, "WF1")]
    SGA = Sem(nc, es, "SGA")
    SGBs = [Sem(nc, es, "SGB0"), Sem(nc, es, "SGB1")]
    SGFs = [Sem(nc, es, "SGF0"), Sem(nc, es, "SGF1")]
    SGR = Sem(nc, es, "SGR")
    SGO = Sem(nc, es, "SGO")

    id_bf = sb("id_bf", [P, P], BF16)
    id_f = sb("id_f", [P, P], F32)
    cwt = sb("cwt", [P, 2, NET, 4], F32)
    cbt = sb("cbt", [P, 2, NET], F32)
    dtbt = sb("dtbt", [P, 2, NET], F32)
    dpt = sb("dpt", [P, 2, NET], F32)
    neglam = sb("neglam", [P, 2, NET, NN], F32)
    nwt = sb("nwt", [P, 2, 8], F32)
    gwt = sb("gwt", [P, 8, 16], F32)
    gbtile = sb("gbtile", [P, 16], F32)
    maskt = sb("maskt", [P, 1], F32)
    selmat = sb("selmat", [32, 32, P], BF16)
    betacol = sb("betacol", [P, SUB], F32)
    zerot = sb("zerot", [P, 512], F32)
    stateV = sb("stateV", [P, NET, NN], F32)
    stateH = sb("stateH", [P, NET, NN], F32)
    recvb = sb("recvb", [P, 512], F32)
    tmtile = sb("tmtile", [P, D], F32)
    tmtile2 = sb("tmtile2", [P, D], F32)
    tmbf = sb("tmbf", [P, D], BF16)
    tmbf2 = sb("tmbf2", [P, D], BF16)
    col1 = sb("col1", [P, 1], F32)
    col2 = sb("col2", [P, 1], F32)
    col3 = sb("col3", [P, 1], F32)
    stg = sb("stg", [P, 2, P], BF16)
    stgf = sb("stgf", [P, 2, P], F32)

    psA = pst("psA", [P, CH])
    psB = pst("psB", [P, CH])
    psBC = pst("psBC", [P, 4 * CH])
    psTb = es.enter_context(nc.psum_tensor("psTb", [P, 2, P], BF16))
    psTf = pst("psTf", [P, 2, P])

    blk = es.enter_context(nc.Block())

    vpar = VE.alloc_register("vpar")
    gpar = GE.alloc_register("gpar")

    @blk.vector
    def _(v):
        for arm in nc.switch_core_id(engines=[VE]):
            VE.reg_mov(vpar, int(arm.logical) & 1)

    @blk.gpsimd
    def _(g):
        for arm in nc.switch_core_id(engines=[GE]):
            GE.reg_mov(gpar, int(arm.logical) & 1)

    @blk.sync
    def _(e):
        for dst, src in [(id_bf, idbf), (id_f, idf), (cwt, cwp), (cbt, cbp),
                         (dtbt, dtbp), (dpt, dpp), (neglam, alogp), (nwt, nwp),
                         (gwt, gwp), (gbtile, gbt), (maskt, maskc),
                         (selmat, selp)]:
            SWA.inc(e.dma_start(out=dst[:], in_=src), 16)

    @blk.scalar
    def _(s):
        SWA.wait(s)
        SS.inc(s.activation(out=neglam[:], in_=neglam[:], func=AF.Exp))

    @blk.vector
    def _(v):
        SS.wait(v)
        v.tensor_scalar(out=neglam[:], in0=neglam[:], scalar1=-1.0,
                        scalar2=None, op0=OP.mult)
        v.memset(betacol[:], BETA)
        SV.inc(v.memset(zerot[:], 0.0))

    @blk.gpsimd
    def _(g):
        SV.wait(g)
        SGA.inc(g.dma_start(out=ccin[0], in_=zerot[:, 0:512]), 16)
        SGA.inc(g.dma_start(out=ccin[2], in_=zerot[:, 0:512]), 16)
        SGA.inc(g.dma_start(out=ccin[1][:, 0:512], in_=zerot[0:3, :]), 16)
        SGA.inc(g.dma_start(out=ccin[1][:, 512:1024], in_=zerot[0:3, :]), 16)
        SGA.inc(g.dma_start(out=r1tm[0:3, 0:512], in_=zerot[0:3, :]), 16)
        SGA.inc(g.dma_start(out=r1tm[0:3, 512:1024], in_=zerot[0:3, :]), 16)
        SGA.inc(g.dma_start(out=r1tm[3 + T:TH, 0:512], in_=zerot[0:125, :]), 16)
        SGA.inc(g.dma_start(out=r1tm[3 + T:TH, 512:1024], in_=zerot[0:125, :]), 16)
        for dst, src in [(wxp, xpwT), (wdt, dtwT), (wout, outwT),
                         (wswg, swg), (wswu, swu), (wswd, swd),
                         (wewg, ewg), (wewu, ewu), (wewd, ewd)]:
            SGA.inc(g.dma_start(out=dst, in_=src), 16)

    with contextlib.ExitStack() as pes:
        inw_tmp = pes.enter_context(nc.sbuf_tensor("inw_tmp", [P, 2, 4096], F32))
        inw_out = pes.enter_context(nc.sbuf_tensor("inw_out", [P, 2, 4096], BF16))
        for l in range(2):
            for c in range(8):
                slot = c % 2
                idx = l * 8 + c

                @blk.sync
                def _(e, l=l, c=c, slot=slot, idx=idx):
                    if idx >= 2:
                        SV.wait(e, SV.v - 1)
                    SWIs[slot].inc(e.dma_start(out=inw_tmp[:, slot, :],
                                               in_=inwT[l, ts(c, P), :]), 16)

                @blk.vector
                def _(v, l=l, c=c, slot=slot, idx=idx):
                    SWIs[slot].wait(v)
                    if idx >= 2:
                        v.wait_ge(SGA.h, SGA.v)
                    SV.inc(v.tensor_scalar(out=inw_out[:, slot, :],
                                           in0=inw_tmp[:, slot, :],
                                           scalar1=nwt[:, l, c:c + 1],
                                           scalar2=None, op0=OP.mult))

                @blk.gpsimd
                def _(g, l=l, c=c, slot=slot):
                    SV.wait(g)
                    SGA.inc(g.dma_start(out=winw[l, ts(c, P), :],
                                       in_=inw_out[:, slot, :]), 16)

    def emit_norm_tile(i, src_dram, do_ln, outs_fm, out_tm=None, extra_tm32=None,
                       wait_rs=None):
        @blk.sync
        def _(e):
            if wait_rs is not None:
                wait_rs.wait(e)
            SGR.wait(e)
            ST.wait(e)
            SWT.inc(e.dma_start(out=tmtile[:], in_=src_dram[ts(i, P), :]), 16)

        @blk.vector
        def _(v):
            SWT.wait(v)
            if do_ln:
                v.reduce_sum(col1[:], tmtile[:], axis=AX.X)
                v.drain()
                v.tensor_scalar(out=col1[:], in0=col1[:], scalar1=1.0 / D,
                                scalar2=None, op0=OP.mult)
                v.drain()
                v.tensor_scalar(out=tmtile[:], in0=tmtile[:], scalar1=col1[:, 0:1],
                                scalar2=None, op0=OP.subtract)
            v.tensor_tensor(out=tmtile2[:], in0=tmtile[:], in1=tmtile[:], op=OP.mult)
            v.reduce_sum(col2[:], tmtile2[:], axis=AX.X)
            v.drain()
            SV.inc(v.tensor_scalar(out=col2[:], in0=col2[:], scalar1=1.0 / D,
                                   scalar2=EPS, op0=OP.mult, op1=OP.add))

        @blk.scalar
        def _(s):
            SV.wait(s)
            SS.inc(s.activation(out=col3[:], in_=col2[:], func=AF.Sqrt))

        @blk.vector
        def _(v):
            SS.wait(v)
            v.reciprocal(col3[:], col3[:])
            v.drain()
            SV.inc(v.tensor_scalar(out=tmtile[:], in0=tmtile[:], scalar1=col3[:, 0:1],
                                   scalar2=None, op0=OP.mult))

        if out_tm is not None:
            @blk.scalar
            def _(s_):
                SV.wait(s_)
                SGR.inc(s_.dma_start(out=out_tm[ts(i, P), :], in_=tmtile[:]), 16)

        if any(w == "norm" for _, w in outs_fm):
            @blk.vector
            def _(v):
                v.tensor_tensor(out=tmtile2[:], in0=tmtile[:], in1=tmtile[:],
                                op=OP.mult)
                v.reduce_sum(col2[:], tmtile2[:], axis=AX.X)
                v.drain()
                SV.inc(v.tensor_scalar(out=col2[:], in0=col2[:], scalar1=1.0 / D,
                                       scalar2=EPS, op0=OP.mult, op1=OP.add))

            @blk.scalar
            def _(s):
                SV.wait(s)
                SS.inc(s.activation(out=col3[:], in_=col2[:], func=AF.Sqrt))

            @blk.vector
            def _(v):
                SS.wait(v)
                v.reciprocal(col3[:], col3[:])
                v.drain()
                SV.inc(v.tensor_scalar(out=tmtile2[:], in0=tmtile[:],
                                       scalar1=col3[:, 0:1], scalar2=None,
                                       op0=OP.mult))

        for oi, (dst_fm, which) in enumerate(outs_fm):
            buf = tmbf if which == "norm" else tmbf2
            srcten = tmtile2 if which == "norm" else tmtile

            @blk.vector
            def _(v, buf=buf, srcten=srcten):
                ST.wait(v)
                SV.inc(v.tensor_copy(buf[:], srcten[:]))

            for c in range(8):
                sl = c % 2

                @blk.tensor
                def _(t, buf=buf, c=c, sl=sl):
                    SV.wait(t)
                    ST.inc(t.transpose(psTb[:, sl, :], buf[:, ts(c, P)], id_bf[:]))

                @blk.vector
                def _(v, c=c, sl=sl):
                    ST.wait(v)
                    SGBs[sl].wait(v)
                    SV.inc(v.tensor_copy(stg[:, sl, :], psTb[:, sl, :]))

                @blk.gpsimd
                def _(g, dst_fm=dst_fm, c=c, sl=sl, i=i):
                    SV.wait(g)
                    SGBs[sl].inc(g.dma_start(out=dst_fm[ts(c, P), ts(i, P)],
                                             in_=stg[:, sl, :]), 16)

        if extra_tm32 is not None:
            for c in range(8):
                sl = c % 2

                @blk.tensor
                def _(t, c=c, sl=sl):
                    SV.wait(t)
                    ST.inc(t.transpose(psTf[:, sl, :], tmtile[:, ts(c, P)], id_f[:]))

                @blk.vector
                def _(v, c=c, sl=sl):
                    ST.wait(v)
                    SGFs[sl].wait(v)
                    SV.inc(v.tensor_copy(stgf[:, sl, :], psTf[:, sl, :]))

                @blk.gpsimd
                def _(g, c=c, sl=sl, i=i):
                    SV.wait(g)
                    SGFs[sl].inc(g.dma_start(out=extra_tm32[ts(c, P), ts(i, P)],
                                             in_=stgf[:, sl, :]), 16)

    for i in range(NTILE):
        emit_norm_tile(i, xh, do_ln=True,
                       outs_fm=[(xnfm, "pre"), (u0fm, "norm")],
                       out_tm=xntm, extra_tm32=xnfm32)

    with contextlib.ExitStack() as mes:
        msb = lambda n, shp, dt: mes.enter_context(nc.sbuf_tensor(n, shp, dt))
        wbufA = msb("wbufA", [P, 2, 8, P], BF16)
        wbufO = msb("wbufO", [P, 2, NET, P], BF16)
        xptiles = msb("xptiles", [P, NET, 96], BF16)
        dttiles = msb("dttiles", [64, NET, P], BF16)
        urhs = msb("urhs", [P, 8, CH], BF16)
        uhalo = msb("uhalo", [P, 8, 3], BF16)
        xzb = msb("xzb", [P, 515], BF16)
        tails = msb("tails", [P, NET, 3], BF16)
        xbt = msb("xbt", [P, NET, CH], BF16)
        szt = msb("szt", [P, NET, CH], BF16)
        delt = msb("delt", [P, NET, CH], BF16)
        convacc = msb("convacc", [P, CH], F32)
        sps = msb("sps", [P, CH], F32)
        drb = msb("drb", [64, CH], BF16)
        bcsrc = msb("bcsrc", [32, CH], BF16)
        bct = msb("bct", [P, 2, NN, CH], BF16)
        wj = msb("wj", [P, CH], BF16)
        dAf = msb("dAf", [P, NN, SUB], BF16)
        hfld = msb("hfld", [P, NN, SUB], BF16)
        mfld = msb("mfld", [P, NN, SUB], BF16)
        vfld = msb("vfld", [P, NN, SUB], BF16)
        ybuf = msb("ybuf", [P, NSUB, SUB], BF16)
        ybf = msb("ybf", [P, CH], BF16)
        uout = msb("uout", [P, NET, CH], BF16)
        mbst = msb("mbst", [P, 8, CH], BF16)

        def emit_state_cc(l, cc_idx):
            ccs = CCS[cc_idx]
            rs = RS[cc_idx]

            @blk.gpsimd
            def _(g):
                SGA.wait(g)
                with g.If_eq(gpar, 0):
                    g.wait_ge(SST.h, l + 1)
                    g.dma_start(out=ccin[cc_idx][:, 0:256],
                                in_=stateV[:].rearrange("p a b -> p (a b)")
                                ).then_inc(GT.h, 16)
                    g.dma_start(out=ccin[cc_idx][:, 256:512],
                                in_=stateH[:].rearrange("p a b -> p (a b)")
                                ).then_inc(GT.h, 16)
                    GT.v += 32
                    g.wait_ge(GT.h, GT.v)
                g.sem_inc(PK.h, 1)
                PK.v += 1
                ccs.inc(g.collective_compute(
                    "AllReduce", mybir.AluOpType.add, replica_groups=PAIRS,
                    ins=[ccin[cc_idx].opt()], outs=[ccout[cc_idx].opt()]))
                ccs.wait(g)
                rs.inc(g.dma_start(out=recvb[:], in_=ccout[cc_idx]), 16)

        def emit_mamba_layer(l, ufm, cc_idx, epilogue):
            emit_state_cc(l, cc_idx)

            @blk.sync
            def _(e, l=l):
                SGA.wait(e)
                for sg_ in SGBs:
                    sg_.wait(e)
                for sg_ in SGFs:
                    sg_.wait(e)
                if l == 0:
                    ST.wait(e)
                else:
                    ST.wait(e)
                SXP.inc(e.dma_start(out=xptiles[:],
                                   in_=wxp[l].rearrange("(j p) r -> p j r", p=P)), 16)
                SXP.inc(e.dma_start(out=dttiles[:],
                                   in_=wdt[l].rearrange("r (j m) -> r j m", m=P)), 16)

            def emit_state_init(v, l, rs):
                if l == 1:
                    v.wait_ge(PK.h, 1)
                v.memset(stateV[:], 0.0)
                SV.inc(v.memset(stateH[:], 0.0))
                with v.If_eq(vpar, 1):
                    v.wait_ge(rs.h, 16)
                    v.tensor_scalar(out=stateV[:].rearrange("p a b -> p (a b)"),
                                    in0=recvb[:, 0:256], scalar1=maskt[:, 0:1],
                                    scalar2=None, op0=OP.mult)
                    v.tensor_scalar(out=stateH[:].rearrange("p a b -> p (a b)"),
                                    in0=recvb[:, 256:512], scalar1=maskt[:, 0:1],
                                    scalar2=None, op0=OP.mult)
                v.drain()

            def emit_post(k):
                for m in range(8):
                    slot = m % 2

                    @blk.sync
                    def _(e, m=m, slot=slot, l=l):
                        if m >= 2:
                            e.wait_ge(ST.h, ST.v - 1)
                        WOs[slot].inc(e.dma_start(
                            out=wbufO[:, slot, :, :],
                            in_=wout[l][:, ts(m, P)]
                            .rearrange("(kc p) m -> p kc m", p=P)), 16)

                    @blk.tensor
                    def _(t, m=m, slot=slot):
                        WOs[slot].wait(t)
                        waitmk(t, "uout")
                        waitmk(t, "psA" if m % 2 == 0 else "psB")
                        ps = psA if m % 2 == 0 else psB
                        for kc in range(NET):
                            mm = t.matmul(ps[:], wbufO[:, slot, kc, :],
                                          uout[:, kc, :],
                                          start=(kc == 0), stop=(kc == NET - 1))
                        ST.inc(mm)

                    @blk.vector
                    def _(v, m=m):
                        ST.wait(v)
                        ps = psA if m % 2 == 0 else psB
                        SV.inc(v.tensor_copy(mbst[:, m, :], ps[:]))
                    setmk("psA" if m % 2 == 0 else "psB", SV)
                setmk("uoutfree", ST)

                for tau in range(4):
                    rowbase = CH * k + P * tau + 3

                    if epilogue == "r1":
                        @blk.sync
                        def _(e, rowbase=rowbase):
                            SGR.wait(e)
                            SWT.inc(e.dma_start(
                                out=tmtile[:], in_=xntm[rowbase:rowbase + P, :]), 16)

                    for c in range(8):
                        sl = c % 2

                        @blk.tensor
                        def _(t, c=c, tau=tau, sl=sl):
                            SV.wait(t)
                            SS.wait(t)
                            ST.inc(t.transpose(psTb[:, sl, :],
                                               mbst[:, c, ts(tau, P)], id_bf[:]))

                        @blk.vector
                        def _(v, c=c, sl=sl, epilogue=epilogue):
                            ST.wait(v)
                            if epilogue != "r1" and c == 0:
                                v.wait_ge(SGR.h, SGR.v)
                            if epilogue == "r1":
                                SWT.wait(v)
                                SV.inc(v.tensor_tensor(
                                    out=tmtile[:, ts(c, P)],
                                    in0=tmtile[:, ts(c, P)],
                                    in1=psTb[:, sl, :], op=OP.add))
                            else:
                                SV.inc(v.tensor_copy(tmtile[:, ts(c, P)],
                                                     psTb[:, sl, :]))

                    setmk("epiadds", SV)

                    @blk.scalar
                    def _(s_, rowbase=rowbase, epilogue=epilogue):
                        waitmk(s_, "epiadds")
                        dst = r1tm if epilogue == "r1" else mb1tm
                        SGR.inc(s_.dma_start(out=dst[rowbase:rowbase + P, :],
                                             in_=tmtile[:]), 16)


            for k in range(NCH):
                c0 = 3 + CH * k

                @blk.sync
                def _(e, k=k, c0=c0, ufm=ufm):
                    ST.wait(e)
                    SWU.inc(e.dma_start(
                        out=urhs[:],
                        in_=ufm.rearrange("(c p) t -> p c t", p=P)[:, :, c0:c0 + CH]), 16)
                    if k == 0:
                        SWU.inc(e.dma_start(
                            out=uhalo[:],
                            in_=ufm.rearrange("(c p) t -> p c t", p=P)[:, :, 0:3]), 16)

                for m in range(32):
                    slot = m % 2

                    @blk.sync
                    def _(e, m=m, slot=slot, l=l):
                        if m >= 2:
                            e.wait_ge(ST.h, ST.v - 1)
                        WAs[slot].inc(e.dma_start(
                            out=wbufA[:, slot, :, :],
                            in_=winw[l][:, ts(m, P)].rearrange("(kc p) m -> p kc m", p=P)), 16)

                    @blk.tensor
                    def _(t, m=m, slot=slot, k=k, l=l):
                        WAs[slot].wait(t)
                        SWU.wait(t)
                        waitmk(t, "psA" if m % 2 == 0 else "psB")
                        if k == 0 and m < 16:
                            waitmk(t, "psBC")
                        ps = psA if m % 2 == 0 else psB
                        for kc in range(8):
                            mm = t.matmul(ps[:], wbufA[:, slot, kc, :], urhs[:, kc, :],
                                          start=(kc == 0), stop=(kc == 7))
                        ST.inc(mm)
                        if k == 0 and m < 16:
                            for kc in range(8):
                                mm = t.matmul(psBC[:, 3 * m:3 * m + 3],
                                              wbufA[:, slot, kc, :], uhalo[:, kc, :],
                                              start=(kc == 0), stop=(kc == 7))
                            ST.inc(mm)

                    ps = psA if m % 2 == 0 else psB
                    if m < 16:
                        @blk.vector
                        def _(v, m=m, ps=ps, k=k, l=l):
                            ST.wait(v)
                            if k == 0:
                                v.tensor_copy(xzb[:, 0:3], psBC[:, 3 * m:3 * m + 3])
                            else:
                                v.tensor_copy(xzb[:, 0:3], tails[:, m, :])
                            v.tensor_copy(xzb[:, 3:515], ps[:])
                            v.tensor_copy(tails[:, m, :], ps[:, 509:512])
                            v.tensor_scalar(out=convacc[:], in0=xzb[:, 0:CH],
                                            scalar1=cwt[:, l, m, 0:1],
                                            scalar2=cbt[:, l, m:m + 1],
                                            op0=OP.mult, op1=OP.add)
                            for kk in range(1, 4):
                                ins_ = v.scalar_tensor_tensor(
                                    out=convacc[:], in0=xzb[:, kk:kk + CH],
                                    scalar=cwt[:, l, m, kk:kk + 1], in1=convacc[:],
                                    op0=OP.mult, op1=OP.add)
                            SV.inc(ins_)

                        setmk("psA" if m % 2 == 0 else "psB", SV)

                        @blk.scalar
                        def _(s, m=m, l=l):
                            SV.wait(s)
                            SS.inc(s.activation(out=xbt[:, m, :], in_=convacc[:],
                                                func=AF.Silu))
                    else:
                        @blk.scalar
                        def _(s, m=m, ps=ps):
                            ST.wait(s)
                            SS.inc(s.activation(out=szt[:, m - 16, :], in_=ps[:],
                                                func=AF.Silu))
                        setmk("psA" if m % 2 == 0 else "psB", SS)
                    if k == 0 and m == 15:
                        setmk("psBC", SV)

                @blk.tensor
                def _(t):
                    SS.wait(t)
                    waitmk(t, "psB")
                    SXP.wait(t)
                    for j in range(NET):
                        mm = t.matmul(psB[0:96, :], xptiles[:, j, :], xbt[:, j, :],
                                      start=(j == 0), stop=(j == NET - 1))
                    ST.inc(mm)

                @blk.vector
                def _(v):
                    ST.wait(v)
                    v.tensor_copy(drb[:], psB[0:64, :])
                    SV.inc(v.tensor_copy(bcsrc[:], psB[64:96, :]))
                setmk("psB", SV)
                setmk("drb", SV)

                for q in range(8):
                    @blk.tensor
                    def _(t, q=q):
                        waitmk(t, "drb")
                        waitmk(t, "psBC")
                        for u in range(4):
                            n = 4 * q + u
                            mm = t.matmul(psBC[:, ts(u, CH)], selmat[:, n, :],
                                          bcsrc[:, :], start=True, stop=True)
                        ST.inc(mm)

                    @blk.vector
                    def _(v, q=q):
                        ST.wait(v)
                        SV.inc(v.tensor_copy(
                            bct[:, q // 4, 4 * (q % 4):4 * (q % 4) + 4, :]
                            .rearrange("p a b -> p (a b)"),
                            psBC[:]))
                    setmk("psBC", SV)

                for j in range(NET):
                    @blk.tensor
                    def _(t, j=j):
                        waitmk(t, "psA")
                        waitmk(t, "drb")
                        ST.inc(t.matmul(psA[:], dttiles[:, j, :], drb[:],
                                        start=True, stop=True))

                    @blk.vector
                    def _(v, j=j, l=l):
                        ST.wait(v)
                        v.tensor_scalar(out=convacc[:], in0=psA[:],
                                        scalar1=dtbt[:, l, j:j + 1],
                                        scalar2=None, op0=OP.add)
                        SV.inc(v.scalar_tensor_tensor(out=sps[:], in0=convacc[:],
                                                      scalar=-1.0, in1=convacc[:],
                                                      op0=OP.mult, op1=OP.min))
                    setmk("psA", SV)

                    @blk.scalar
                    def _(s, j=j):
                        SV.wait(s)
                        s.activation(out=sps[:], in_=sps[:], func=AF.Exp)
                        s.drain()
                        SS.inc(s.activation(out=sps[:], in_=sps[:], func=AF.Ln,
                                            bias=1.0))

                    @blk.vector
                    def _(v, j=j):
                        SS.wait(v)
                        v.scalar_tensor_tensor(out=convacc[:], in0=convacc[:],
                                               scalar=0.0, in1=sps[:],
                                               op0=OP.max, op1=OP.add)
                        SV.inc(v.tensor_copy(delt[:, j, :], convacc[:]))

                if k > 0:
                    emit_post(k - 1)

                if k == 0:
                    @blk.vector
                    def _(v, l=l, rs=RS[cc_idx]):
                        emit_state_init(v, l, rs)

                for j in range(NET):
                    @blk.vector
                    def _(v, j=j):
                        SS.wait(v)
                        SV.inc(v.tensor_tensor(out=wj[:], in0=delt[:, j, :],
                                               in1=xbt[:, j, :], op=OP.mult))
                    for s_ in range(NSUB):
                        t0 = SUB * s_

                        @blk.vector
                        def _(v, j=j, t0=t0, l=l):
                            v.tensor_tensor(
                                out=mfld[:],
                                in0=wj[:, t0:t0 + SUB].unsqueeze(1)
                                .broadcast_to([P, NN, SUB]),
                                in1=bct[:, 0, :, t0:t0 + SUB], op=OP.mult)
                            for n in range(NN):
                                ins_ = v.tensor_scalar(
                                    out=dAf[:, n, :], in0=delt[:, j, t0:t0 + SUB],
                                    scalar1=neglam[:, l, j, n:n + 1],
                                    scalar2=None, op0=OP.mult)
                            SV.inc(ins_)

                        @blk.scalar
                        def _(s):
                            SV.wait(s)
                            SS.inc(s.activation(
                                out=dAf[:].rearrange("p a b -> p (a b)"),
                                in_=dAf[:].rearrange("p a b -> p (a b)"),
                                func=AF.Exp))

                        @blk.vector
                        def _(v, j=j, t0=t0, s_=s_):
                            SS.wait(v)
                            for n in range(NN):
                                v.tensor_tensor_scan(vfld[:, n, :], betacol[:],
                                                     mfld[:, n, :],
                                                     stateV[:, j, n:n + 1],
                                                     OP.mult, OP.add)
                            for n in range(NN):
                                v.tensor_tensor_scan(hfld[:, n, :], dAf[:, n, :],
                                                     vfld[:, n, :],
                                                     stateH[:, j, n:n + 1],
                                                     OP.mult, OP.add)
                            v.tensor_copy(stateV[:, j, :], vfld[:, :, SUB - 1])
                            v.tensor_tensor(out=mfld[:], in0=hfld[:],
                                            in1=bct[:, 1, :, t0:t0 + SUB], op=OP.mult)
                            v.tensor_tensor(out=mfld[:, 0:8, :], in0=mfld[:, 0:8, :],
                                            in1=mfld[:, 8:16, :], op=OP.add)
                            v.tensor_tensor(out=mfld[:, 0:4, :], in0=mfld[:, 0:4, :],
                                            in1=mfld[:, 4:8, :], op=OP.add)
                            v.tensor_tensor(out=mfld[:, 0:2, :], in0=mfld[:, 0:2, :],
                                            in1=mfld[:, 2:4, :], op=OP.add)
                            v.tensor_tensor(out=ybuf[:, s_, :],
                                            in0=mfld[:, 0, :],
                                            in1=mfld[:, 1, :], op=OP.add)
                            SV.inc(v.tensor_copy(stateH[:, j, :],
                                                 hfld[:, :, SUB - 1]))

                    @blk.vector
                    def _(v, j=j, l=l):
                        waitmk(v, "uoutfree")
                        v.scalar_tensor_tensor(
                            out=ybf[:], in0=xbt[:, j, :], scalar=dpt[:, l, j:j + 1],
                            in1=ybuf[:].rearrange("p a b -> p (a b)"),
                            op0=OP.mult, op1=OP.add)
                        SV.inc(v.tensor_tensor(out=uout[:, j, :], in0=ybf[:],
                                               in1=szt[:, j, :], op=OP.mult))

                setmk("uout", SV)

                if k == NCH - 1:
                    @blk.vector
                    def _(v):
                        SST.inc(v.engine_nop())

            emit_post(NCH - 1)

            if epilogue == "r1":
                @blk.gpsimd
                def _(g):
                    with g.If_eq(gpar, 0):
                        g.wait_ge(SGR.h, SGR.v)
                        g.dma_start(out=ccin[1],
                                    in_=r1tm[T:T + 3, :]).then_inc(GT.h, 16)
                        GT.v += 16
                        g.wait_ge(GT.h, GT.v)
                    CCS[1].inc(g.collective_compute(
                        "AllReduce", mybir.AluOpType.add, replica_groups=PAIRS,
                        ins=[ccin[1].opt()], outs=[ccout[1].opt()]))
                    CCS[1].wait(g)
                    with g.If_eq(gpar, 1):
                        g.dma_start(out=r1tm[0:3, :], in_=ccout[1]).then_inc(RS[1].h, 16)
                    with g.If_eq(gpar, 0):
                        g.dma_start(out=r1tm[0:3, 0:512],
                                    in_=zerot[0:3, :]).then_inc(RS[1].h, 16)
                    RS[1].v += 16

        emit_mamba_layer(0, u0fm, 0, "r1")


        for i in range(NTILE):
            emit_norm_tile(i, r1tm, do_ln=False, outs_fm=[(u1fm, "pre")],
                           wait_rs=(RS[1] if i == 0 else None))

        emit_mamba_layer(1, u1fm, 2, "mb1")

    with contextlib.ExitStack() as fes:
        fsb = lambda n, shp, dt: fes.enter_context(nc.sbuf_tensor(n, shp, dt))
        xnrhs = fsb("xnrhs", [P, 8, CH], BF16)
        xnrhs32 = fsb("xnrhs32", [P, 8, P], F32)
        wgbuf = fsb("wgbuf", [P, 2, 8, 4, P], BF16)
        wubuf = fsb("wubuf", [P, 2, 8, 4, P], BF16)
        wdbuf = fsb("wdbuf", [P, 2, 4, 8, P], BF16)
        sgb = fsb("sgb", [P, 4, CH], BF16)
        gtd = fsb("gtd", [P, 4, CH], BF16)
        wbc = fsb("wbc", [P, CH], BF16)
        expwt = fsb("expwt", [P, 8, CH], F32)
        wtsfm = fsb("wtsfm", [17, T], BF16)
        rtile = fsb("rtile", [P, 16], F32)
        rb2 = fsb("rb2", [P, 16], F32)
        rb3 = fsb("rb3", [P, 16], F32)
        rcol = fsb("rcol", [P, 1], F32)
        rcol2 = fsb("rcol2", [P, 1], F32)

        @blk.sync
        def _(e):
            SWX.inc(e.dma_start(out=wtsfm[16:17, :], in_=onesd), 16)

        for i in range(16):
            tok0 = P * i

            @blk.sync
            def _(e, tok0=tok0):
                for sg_ in SGFs:
                    sg_.wait(e)
                ST.wait(e)
                SWX.inc(e.dma_start(
                    out=xnrhs32[:],
                    in_=xnfm32.rearrange("(c p) t -> p c t", p=P)
                    [:, :, 3 + tok0:3 + tok0 + P]), 16)

            @blk.tensor
            def _(t):
                SWX.wait(t)
                t.wait_ge(SV.h, SV.v)
                for kc in range(8):
                    mm = t.matmul(psA[0:16, 0:P], gwt[:, kc, :], xnrhs32[:, kc, :],
                                  start=(kc == 0), stop=(kc == 7))
                ST.inc(mm)

            @blk.vector
            def _(v):
                ST.wait(v)
                SV.inc(v.tensor_copy(stgf[:, 0, :][0:16, :], psA[0:16, 0:P]))

            @blk.tensor
            def _(t):
                SV.wait(t)
                ST.inc(t.transpose(psTf[:, 0, :][:, 0:16], stgf[:, 0, :][0:16, :],
                                   id_f[0:16, 0:16]))

            @blk.vector
            def _(v):
                ST.wait(v)
                SV.inc(v.tensor_tensor(out=rtile[:], in0=psTf[:, 0, 0:16],
                                       in1=gbtile[:], op=OP.add))

            @blk.scalar
            def _(s):
                SV.wait(s)
                SS.inc(s.activation(out=rtile[:], in_=rtile[:], func=AF.Sigmoid))

            @blk.vector
            def _(v):
                SS.wait(v)
                v.reduce_max(rcol[:], rtile[:], axis=AX.X)
                v.drain()
                SV.inc(v.tensor_scalar(out=rb2[:], in0=rtile[:],
                                       scalar1=rcol[:, 0:1], scalar2=None,
                                       op0=OP.subtract))
                v.drain()

            @blk.scalar
            def _(s):
                SV.wait(s)
                SS.inc(s.activation(out=rb2[:], in_=rb2[:], func=AF.Sign, scale=-1.0))
                s.drain()

            @blk.vector
            def _(v):
                SS.wait(v)
                v.tensor_tensor(out=rb3[:], in0=rtile[:], in1=rb2[:], op=OP.mult)
                v.drain()
                v.tensor_scalar(out=rb2[:], in0=rb2[:], scalar1=100.0,
                                scalar2=-100.0, op0=OP.mult, op1=OP.add)
                v.drain()
                v.tensor_tensor(out=rb3[:], in0=rb3[:], in1=rb2[:], op=OP.add)
                v.drain()
                v.reduce_max(rcol2[:], rb3[:], axis=AX.X)
                v.drain()
                SV.inc(v.tensor_scalar(out=rb2[:], in0=rtile[:],
                                       scalar1=rcol2[:, 0:1], scalar2=None,
                                       op0=OP.subtract))
                v.drain()

            @blk.scalar
            def _(s):
                SV.wait(s)
                SS.inc(s.activation(out=rb2[:], in_=rb2[:], func=AF.Sign))
                s.drain()

            @blk.vector
            def _(v):
                SS.wait(v)
                v.tensor_scalar(out=rb2[:], in0=rb2[:], scalar1=1.0, scalar2=1.0,
                                op0=OP.add, op1=OP.min)
                v.drain()
                v.tensor_tensor(out=rb3[:], in0=rtile[:], in1=rb2[:], op=OP.mult)
                v.drain()
                v.tensor_tensor(out=rcol[:], in0=rcol[:], in1=rcol2[:], op=OP.add)
                v.drain()
                v.reciprocal(rcol[:], rcol[:])
                v.drain()
                SV.inc(v.tensor_scalar(out=rb3[:], in0=rb3[:], scalar1=rcol[:, 0:1],
                                       scalar2=None, op0=OP.mult))
                v.drain()

            @blk.tensor
            def _(t):
                SV.wait(t)
                ST.inc(t.transpose(psTf[:, 0, :][0:16, :], rb3[:], id_f[:]))

            @blk.vector
            def _(v, i=i):
                ST.wait(v)
                SV.inc(v.tensor_copy(wtsfm[0:16, ts(i, P)], psTf[0:16, 0, :]))

        for k in range(NCH):
            c0 = 3 + CH * k

            @blk.sync
            def _(e, c0=c0):
                ST.wait(e)
                for sg_ in SGBs:
                    sg_.wait(e)
                SWX.inc(e.dma_start(
                    out=xnrhs[:],
                    in_=xnfm.rearrange("(c p) t -> p c t", p=P)[:, :, c0:c0 + CH]), 16)

            @blk.vector
            def _(v):
                SWX.wait(v)
                SV.inc(v.memset(expwt[:], 0.0))

            for e_ in range(17):
                slot = e_ % 2

                @blk.sync
                def _(e, e_=e_, slot=slot, k=k):
                    if e_ >= 2 or k > 0:
                        e.wait_ge(ST.h, ST.v - 2)
                    srcg = wewg[e_] if e_ < 16 else wswg
                    srcu = wewu[e_] if e_ < 16 else wswu
                    srcd = wewd[e_] if e_ < 16 else wswd
                    WFs[slot].inc(e.dma_start(
                        out=wgbuf[:, slot, :, :, :],
                        in_=srcg.rearrange("(kc p) (mc m) -> p kc mc m", p=P, m=P)), 16)
                    WFs[slot].inc(e.dma_start(
                        out=wubuf[:, slot, :, :, :],
                        in_=srcu.rearrange("(kc p) (mc m) -> p kc mc m", p=P, m=P)), 16)
                    WFs[slot].inc(e.dma_start(
                        out=wdbuf[:, slot, :, :, :],
                        in_=srcd.rearrange("(kc p) (mc m) -> p kc mc m", p=P, m=P)), 16)

                @blk.tensor
                def _(t, e_=e_, k=k):
                    SV.wait(t)
                    SWX.wait(t)
                    t.wait_ge(SS.h, SS.v)
                    ST.inc(t.matmul(psBC[:, 0:CH], selmat[0:17, e_, :],
                                    wtsfm[:, ts(k, CH)],
                                    start=True, stop=True))

                @blk.vector
                def _(v):
                    ST.wait(v)
                    SV.inc(v.tensor_copy(wbc[:], psBC[:, 0:CH]))

                for mi in range(4):
                    @blk.tensor
                    def _(t, e_=e_, slot=slot, mi=mi):
                        WFs[slot].wait(t)
                        SWX.wait(t)
                        t.wait_ge(SS.h, SS.v)
                        t.wait_ge(SV.h, SV.v)
                        for kc in range(8):
                            mm = t.matmul(psA[:], wgbuf[:, slot, kc, mi, :],
                                          xnrhs[:, kc, :],
                                          start=(kc == 0), stop=(kc == 7))
                        ST.inc(mm)

                    @blk.scalar
                    def _(s, mi=mi):
                        ST.wait(s)
                        SS.inc(s.activation(out=sgb[:, mi, :], in_=psA[:],
                                            func=AF.Silu))

                    @blk.tensor
                    def _(t, e_=e_, slot=slot, mi=mi):
                        t.wait_ge(SS.h, SS.v)
                        t.wait_ge(SV.h, SV.v)
                        for kc in range(8):
                            mm = t.matmul(psB[:], wubuf[:, slot, kc, mi, :],
                                          xnrhs[:, kc, :],
                                          start=(kc == 0), stop=(kc == 7))
                        ST.inc(mm)

                    @blk.vector
                    def _(v, mi=mi):
                        ST.wait(v)
                        SS.wait(v)
                        v.tensor_tensor(out=gtd[:, mi, :], in0=sgb[:, mi, :],
                                        in1=psB[:], op=OP.mult)
                        SV.inc(v.tensor_tensor(out=gtd[:, mi, :], in0=gtd[:, mi, :],
                                               in1=wbc[:], op=OP.mult))


                for md in range(8):
                    @blk.tensor
                    def _(t, e_=e_, slot=slot, md=md):
                        t.wait_ge(SV.h, SV.v)
                        t.wait_ge(SS.h, SS.v)
                        ps = psA if md % 2 == 0 else psB
                        for kc in range(4):
                            mm = t.matmul(ps[:], wdbuf[:, slot, kc, md, :],
                                          gtd[:, kc, :],
                                          start=(kc == 0), stop=(kc == 3))
                        ST.inc(mm)

                    @blk.vector
                    def _(v, md=md):
                        ST.wait(v)
                        ps = psA if md % 2 == 0 else psB
                        SV.inc(v.tensor_tensor(out=expwt[:, md, :],
                                               in0=expwt[:, md, :],
                                               in1=ps[:], op=OP.add))

            for tau in range(4):
                tok0 = CH * k + P * tau

                @blk.sync
                def _(e, tok0=tok0, k=k, tau=tau):
                    SGR.wait(e)
                    SGO.wait(e)
                    SWT.inc(e.dma_start(out=tmtile[:],
                                        in_=xh[3 + tok0:3 + tok0 + P, :]), 16)
                    SWT.inc(e.dma_start(out=tmtile2[:],
                                        in_=xntm[3 + tok0:3 + tok0 + P, :]), 16)

                @blk.vector
                def _(v):
                    SWT.wait(v)
                    SV.inc(v.tensor_tensor(out=tmtile[:], in0=tmtile[:],
                                           in1=tmtile2[:], op=OP.add))

                @blk.sync
                def _(e, tok0=tok0):
                    SV.wait(e)
                    SWT.inc(e.dma_start(out=tmtile2[:],
                                        in_=r1tm[3 + tok0:3 + tok0 + P, :]), 16)

                @blk.vector
                def _(v):
                    SWT.wait(v)
                    SV.inc(v.tensor_tensor(out=tmtile[:], in0=tmtile[:],
                                           in1=tmtile2[:], op=OP.add))

                @blk.sync
                def _(e, tok0=tok0):
                    SV.wait(e)
                    SWT.inc(e.dma_start(out=tmtile2[:],
                                        in_=mb1tm[3 + tok0:3 + tok0 + P, :]), 16)

                @blk.vector
                def _(v):
                    SWT.wait(v)
                    SV.inc(v.tensor_tensor(out=tmtile[:], in0=tmtile[:],
                                           in1=tmtile2[:], op=OP.add))

                for c in range(8):
                    sl = c % 2

                    @blk.tensor
                    def _(t, c=c, tau=tau, sl=sl):
                        SV.wait(t)
                        SS.wait(t)
                        ST.inc(t.transpose(psTf[:, sl, :], expwt[:, c, ts(tau, P)],
                                           id_f[:]))

                    @blk.vector
                    def _(v, c=c, sl=sl):
                        ST.wait(v)
                        SV.inc(v.tensor_tensor(out=tmtile[:, ts(c, P)],
                                               in0=tmtile[:, ts(c, P)],
                                               in1=psTf[:, sl, :], op=OP.add))

                @blk.gpsimd
                def _(g, tok0=tok0):
                    SV.wait(g)
                    SGO.inc(g.dma_start(out=out[tok0:tok0 + P, :], in_=tmtile[:]), 16)

    @blk.gpsimd
    def _(g):
        SGO.wait(g)

    es.close()
    return nc


_CACHE = {}


def kernel(L0_attn_x, L0_attn_norm_w, L0_attn_in_proj_w, L0_attn_conv_w,
           L0_attn_conv_b, L0_attn_x_proj_w, L0_attn_dt_w, L0_attn_dt_b,
           L0_attn_A_log, L0_attn_Dp, L0_attn_out_w, L0_ffn_gate_w,
           L0_ffn_gate_b, L0_ffn_shared_wg, L0_ffn_shared_wu, L0_ffn_shared_wd,
           L0_ffn_experts_wg, L0_ffn_experts_wu, L0_ffn_experts_wd):
    f32 = lambda a: np.ascontiguousarray(np.asarray(a, dtype=np.float32))
    x = f32(L0_attn_x)
    B, L, D_ = x.shape

    common = {
        "inwT": f32(np.transpose(f32(L0_attn_in_proj_w), (0, 2, 1))),
        "xpwT": f32(np.transpose(f32(L0_attn_x_proj_w), (0, 2, 1))),
        "dtwT": f32(np.transpose(f32(L0_attn_dt_w), (0, 2, 1))),
        "outwT": f32(np.transpose(f32(L0_attn_out_w), (0, 2, 1))),
        "cwp": f32(np.transpose(np.reshape(f32(L0_attn_conv_w), (2, NET, P, 4)), (2, 0, 1, 3))),
        "cbp": f32(np.transpose(np.reshape(f32(L0_attn_conv_b), (2, NET, P)), (2, 0, 1))),
        "dtbp": f32(np.transpose(np.reshape(f32(L0_attn_dt_b), (2, NET, P)), (2, 0, 1))),
        "dpp": f32(np.transpose(np.reshape(f32(L0_attn_Dp), (2, NET, P)), (2, 0, 1))),
        "alogp": f32(np.transpose(np.reshape(f32(L0_attn_A_log), (2, NET, P, NN)), (2, 0, 1, 3))),
        "nwp": f32(np.transpose(np.reshape(f32(L0_attn_norm_w), (2, 8, P)), (2, 0, 1))),
        "gwp": f32(np.reshape(f32(L0_ffn_gate_w), (8, P, 16)).transpose(1, 0, 2)),
        "gbt": f32(np.broadcast_to(f32(L0_ffn_gate_b), (P, 16))),
        "maskc": None,
        "idbf": np.eye(P, dtype=BFnp),
        "idf": np.eye(P, dtype=np.float32),
        "selp": np.ascontiguousarray(
            np.transpose(np.eye(32, dtype=BFnp)[:, :, None]
                         * np.ones((1, 1, P), dtype=BFnp), (0, 1, 2))),
        "onesd": np.ones((1, T), dtype=BFnp),
        "swg": f32(L0_ffn_shared_wg), "swu": f32(L0_ffn_shared_wu),
        "swd": f32(L0_ffn_shared_wd),
        "ewg": f32(L0_ffn_experts_wg), "ewu": f32(L0_ffn_experts_wu),
        "ewd": f32(L0_ffn_experts_wd),
    }

    in_maps = []
    for c in range(8):
        b = c // 2
        odd = c % 2
        start = T * odd
        xhc = np.zeros((TH, D_), np.float32)
        if odd:
            xhc[0:3] = x[b, start - 3:start]
        xhc[3:3 + T] = x[b, start:start + T]
        m = dict(common)
        m["xh"] = xhc
        m["maskc"] = np.full((P, 1), float(odd), np.float32)
        in_maps.append(m)

    if "nc" not in _CACHE:
        _CACHE["nc"] = build()
    nc = _CACHE["nc"]
    res = run_bass_kernel_spmd(nc, in_maps, core_ids=list(range(8)))
    outp = np.empty((B, L, D_), np.float32)
    for c in range(8):
        b = c // 2
        odd = c % 2
        outp[b, T * odd:T * odd + T] = np.asarray(res.results[c]["out"])
    return outp


# version 21 (final)
# speedup vs baseline: 1.0240x; 1.0240x over previous
import contextlib
import numpy as np
import ml_dtypes

import concourse.bass as bass
import concourse.mybir as mybir
from concourse.bass_utils import run_bass_kernel_spmd
from concourse.alu_op_type import AluOpType as OP

F32 = mybir.dt.float32
BF16 = mybir.dt.bfloat16
AF = mybir.ActivationFunctionType
AX = mybir.AxisListType
ts = bass.ts

BFnp = ml_dtypes.bfloat16

P = 128
T = 2048
TH = 2176
NTILE = TH // P
CH = 512
NCH = T // CH
SUB = 256
NSUB = CH // SUB
NET = 16
NN = 16
D = 1024
E = 2048
EPS = 1e-5
BETA = 0.6


class Sem:
    def __init__(self, nc, es, name):
        self.h = es.enter_context(nc.semaphore(name))
        self.v = 0

    def inc(self, instr, k=1):
        instr.then_inc(self.h, k)
        self.v += k
        return self.v

    def bump(self, k=1):
        self.v += k
        return self.v

    def wait(self, eng, v=None):
        eng.wait_ge(self.h, self.v if v is None else v)


def build():
    MK = {}

    def setmk(name, sem):
        MK[name] = (sem.h, sem.v)

    def waitmk(eng, name):
        if name in MK:
            h, v = MK[name]
            eng.wait_ge(h, v)

    nc = bass.Bass("TRN2", target_bir_lowering=False,
                   detect_race_conditions=False)
    ap = lambda t: t.ap()

    xh = ap(nc.dram_tensor("xh", [TH, D], F32, kind="ExternalInput"))
    inwT = ap(nc.dram_tensor("inwT", [2, D, 4096], F32, kind="ExternalInput"))
    xpwT = ap(nc.dram_tensor("xpwT", [2, E, 96], F32, kind="ExternalInput"))
    dtwT = ap(nc.dram_tensor("dtwT", [2, 64, E], F32, kind="ExternalInput"))
    outwT = ap(nc.dram_tensor("outwT", [2, E, D], F32, kind="ExternalInput"))
    cwp = ap(nc.dram_tensor("cwp", [P, 2, NET, 4], F32, kind="ExternalInput"))
    cbp = ap(nc.dram_tensor("cbp", [P, 2, NET], F32, kind="ExternalInput"))
    dtbp = ap(nc.dram_tensor("dtbp", [P, 2, NET], F32, kind="ExternalInput"))
    dpp = ap(nc.dram_tensor("dpp", [P, 2, NET], F32, kind="ExternalInput"))
    alogp = ap(nc.dram_tensor("alogp", [P, 2, NET, NN], F32, kind="ExternalInput"))
    nwp = ap(nc.dram_tensor("nwp", [P, 2, 8], F32, kind="ExternalInput"))
    gwp = ap(nc.dram_tensor("gwp", [P, 8, 16], F32, kind="ExternalInput"))
    gbt = ap(nc.dram_tensor("gbt", [P, 16], F32, kind="ExternalInput"))
    maskc = ap(nc.dram_tensor("maskc", [P, 1], F32, kind="ExternalInput"))
    idbf = ap(nc.dram_tensor("idbf", [P, P], BF16, kind="ExternalInput"))
    idf = ap(nc.dram_tensor("idf", [P, P], F32, kind="ExternalInput"))
    selp = ap(nc.dram_tensor("selp", [32, 32, P], BF16, kind="ExternalInput"))
    onesd = ap(nc.dram_tensor("onesd", [1, T], BF16, kind="ExternalInput"))
    swg = ap(nc.dram_tensor("swg", [D, 512], F32, kind="ExternalInput"))
    swu = ap(nc.dram_tensor("swu", [D, 512], F32, kind="ExternalInput"))
    swd = ap(nc.dram_tensor("swd", [512, D], F32, kind="ExternalInput"))
    ewg = ap(nc.dram_tensor("ewg", [16, D, 512], F32, kind="ExternalInput"))
    ewu = ap(nc.dram_tensor("ewu", [16, D, 512], F32, kind="ExternalInput"))
    ewd = ap(nc.dram_tensor("ewd", [16, 512, D], F32, kind="ExternalInput"))

    out = ap(nc.dram_tensor("out", [T, D], F32, kind="ExternalOutput"))

    dram = lambda n, shp, dt: ap(nc.dram_tensor(n, shp, dt, kind="Internal"))
    winw = dram("winw", [2, D, 4096], BF16)
    wxp = dram("wxp", [2, E, 96], BF16)
    wdt = dram("wdt", [2, 64, E], BF16)
    wout = dram("wout", [2, E, D], BF16)
    wswg = dram("wswg", [D, 512], BF16)
    wswu = dram("wswu", [D, 512], BF16)
    wswd = dram("wswd", [512, D], BF16)
    wewg = dram("wewg", [16, D, 512], BF16)
    wewu = dram("wewu", [16, D, 512], BF16)
    wewd = dram("wewd", [16, 512, D], BF16)
    u0fm = dram("u0fm", [D, TH], BF16)
    u1fm = dram("u1fm", [D, TH], BF16)
    xnfm = dram("xnfm", [D, TH], BF16)
    xnfm32 = dram("xnfm32", [D, TH], F32)
    xntm = dram("xntm", [TH, D], F32)
    r1tm = dram("r1tm", [TH, D], F32)
    mb1tm = dram("mb1tm", [TH, D], F32)
    ccin = [dram("cc1in", [P, 512], F32), dram("cc2in", [3, D], F32),
            dram("cc3in", [P, 512], F32)]
    ccout = [dram("cc1out", [P, 512], F32), dram("cc2out", [3, D], F32),
             dram("cc3out", [P, 512], F32)]
    PAIRS = [[0, 1], [2, 3], [4, 5], [6, 7]]

    es = contextlib.ExitStack()
    sb = lambda n, shp, dt: es.enter_context(nc.sbuf_tensor(n, shp, dt))
    pst = lambda n, shp, dt=F32: es.enter_context(nc.psum_tensor(n, shp, dt))

    SY, TE, VE, SE, GE = nc.sync, nc.tensor, nc.vector, nc.scalar, nc.gpsimd

    ST = Sem(nc, es, "ST")
    SV = Sem(nc, es, "SV")
    SS = Sem(nc, es, "SS")
    SST = Sem(nc, es, "SST")
    PK = Sem(nc, es, "PK")
    CCS = [Sem(nc, es, f"CC{i}") for i in (1, 2, 3)]
    RS = [Sem(nc, es, f"RS{i}") for i in (1, 2, 3)]
    GT = Sem(nc, es, "GT")
    SWA = Sem(nc, es, "SWA")
    SWIs = [Sem(nc, es, "SWI0"), Sem(nc, es, "SWI1")]
    SWU = Sem(nc, es, "SWU")
    SWT = Sem(nc, es, "SWT")
    SXP = Sem(nc, es, "SXP")
    SWX = Sem(nc, es, "SWX")
    WAs = [Sem(nc, es, "WA0"), Sem(nc, es, "WA1")]
    WOs = [Sem(nc, es, "WO0"), Sem(nc, es, "WO1")]
    WFs = [Sem(nc, es, "WF0"), Sem(nc, es, "WF1")]
    SGA = Sem(nc, es, "SGA")
    SGBs = [Sem(nc, es, "SGB0"), Sem(nc, es, "SGB1")]
    SGFs = [Sem(nc, es, "SGF0"), Sem(nc, es, "SGF1")]
    SGR = Sem(nc, es, "SGR")
    SGO = Sem(nc, es, "SGO")

    id_bf = sb("id_bf", [P, P], BF16)
    id_f = sb("id_f", [P, P], F32)
    cwt = sb("cwt", [P, 2, NET, 4], F32)
    cbt = sb("cbt", [P, 2, NET], F32)
    dtbt = sb("dtbt", [P, 2, NET], F32)
    dpt = sb("dpt", [P, 2, NET], F32)
    neglam = sb("neglam", [P, 2, NET, NN], F32)
    nwt = sb("nwt", [P, 2, 8], F32)
    gwt = sb("gwt", [P, 8, 16], F32)
    gbtile = sb("gbtile", [P, 16], F32)
    maskt = sb("maskt", [P, 1], F32)
    selmat = sb("selmat", [32, 32, P], BF16)
    betacol = sb("betacol", [P, SUB], F32)
    zerot = sb("zerot", [P, 512], F32)
    stateV = sb("stateV", [P, NET, NN], F32)
    stateH = sb("stateH", [P, NET, NN], F32)
    recvb = sb("recvb", [P, 512], F32)
    tmtile = sb("tmtile", [P, D], F32)
    tmtile2 = sb("tmtile2", [P, D], F32)
    tmbf = sb("tmbf", [P, D], BF16)
    tmbf2 = sb("tmbf2", [P, D], BF16)
    col1 = sb("col1", [P, 1], F32)
    col2 = sb("col2", [P, 1], F32)
    col3 = sb("col3", [P, 1], F32)
    stg = sb("stg", [P, 2, P], BF16)
    stgf = sb("stgf", [P, 2, P], F32)

    psA = pst("psA", [P, CH])
    psB = pst("psB", [P, CH])
    psBC = pst("psBC", [P, 4 * CH])
    psTb = es.enter_context(nc.psum_tensor("psTb", [P, 2, P], BF16))
    psTf = pst("psTf", [P, 2, P])

    blk = es.enter_context(nc.Block())

    vpar = VE.alloc_register("vpar")
    gpar = GE.alloc_register("gpar")

    @blk.vector
    def _(v):
        for arm in nc.switch_core_id(engines=[VE]):
            VE.reg_mov(vpar, int(arm.logical) & 1)

    @blk.gpsimd
    def _(g):
        for arm in nc.switch_core_id(engines=[GE]):
            GE.reg_mov(gpar, int(arm.logical) & 1)

    @blk.sync
    def _(e):
        for dst, src in [(id_bf, idbf), (id_f, idf), (cwt, cwp), (cbt, cbp),
                         (dtbt, dtbp), (dpt, dpp), (neglam, alogp), (nwt, nwp),
                         (gwt, gwp), (gbtile, gbt), (maskt, maskc),
                         (selmat, selp)]:
            SWA.inc(e.dma_start(out=dst[:], in_=src), 16)

    @blk.scalar
    def _(s):
        SWA.wait(s)
        SS.inc(s.activation(out=neglam[:], in_=neglam[:], func=AF.Exp))

    @blk.vector
    def _(v):
        SS.wait(v)
        v.tensor_scalar(out=neglam[:], in0=neglam[:], scalar1=-1.0,
                        scalar2=None, op0=OP.mult)
        v.memset(betacol[:], BETA)
        SV.inc(v.memset(zerot[:], 0.0))

    @blk.gpsimd
    def _(g):
        SV.wait(g)
        SGA.inc(g.dma_start(out=ccin[0], in_=zerot[:, 0:512]), 16)
        SGA.inc(g.dma_start(out=ccin[2], in_=zerot[:, 0:512]), 16)
        SGA.inc(g.dma_start(out=ccin[1][:, 0:512], in_=zerot[0:3, :]), 16)
        SGA.inc(g.dma_start(out=ccin[1][:, 512:1024], in_=zerot[0:3, :]), 16)
        SGA.inc(g.dma_start(out=r1tm[0:3, 0:512], in_=zerot[0:3, :]), 16)
        SGA.inc(g.dma_start(out=r1tm[0:3, 512:1024], in_=zerot[0:3, :]), 16)
        SGA.inc(g.dma_start(out=r1tm[3 + T:TH, 0:512], in_=zerot[0:125, :]), 16)
        SGA.inc(g.dma_start(out=r1tm[3 + T:TH, 512:1024], in_=zerot[0:125, :]), 16)
        for dst, src in [(wxp, xpwT), (wdt, dtwT), (wout, outwT),
                         (wswg, swg), (wswu, swu), (wswd, swd),
                         (wewg, ewg), (wewu, ewu), (wewd, ewd)]:
            SGA.inc(g.dma_start(out=dst, in_=src), 16)

    with contextlib.ExitStack() as pes:
        inw_tmp = pes.enter_context(nc.sbuf_tensor("inw_tmp", [P, 2, 4096], F32))
        inw_out = pes.enter_context(nc.sbuf_tensor("inw_out", [P, 2, 4096], BF16))
        for l in range(2):
            for c in range(8):
                slot = c % 2
                idx = l * 8 + c

                @blk.sync
                def _(e, l=l, c=c, slot=slot, idx=idx):
                    if idx >= 2:
                        SV.wait(e, SV.v - 1)
                    SWIs[slot].inc(e.dma_start(out=inw_tmp[:, slot, :],
                                               in_=inwT[l, ts(c, P), :]), 16)

                @blk.vector
                def _(v, l=l, c=c, slot=slot, idx=idx):
                    SWIs[slot].wait(v)
                    if idx >= 2:
                        v.wait_ge(SGA.h, SGA.v)
                    SV.inc(v.tensor_scalar(out=inw_out[:, slot, :],
                                           in0=inw_tmp[:, slot, :],
                                           scalar1=nwt[:, l, c:c + 1],
                                           scalar2=None, op0=OP.mult))

                @blk.gpsimd
                def _(g, l=l, c=c, slot=slot):
                    SV.wait(g)
                    SGA.inc(g.dma_start(out=winw[l, ts(c, P), :],
                                       in_=inw_out[:, slot, :]), 16)

    def emit_norm_tile(i, src_dram, do_ln, outs_fm, out_tm=None, extra_tm32=None,
                       wait_rs=None):
        @blk.sync
        def _(e):
            if wait_rs is not None:
                wait_rs.wait(e)
            SGR.wait(e)
            ST.wait(e)
            SWT.inc(e.dma_start(out=tmtile[:], in_=src_dram[ts(i, P), :]), 16)

        @blk.vector
        def _(v):
            SWT.wait(v)
            if do_ln:
                v.reduce_sum(col1[:], tmtile[:], axis=AX.X)
                v.drain()
                v.tensor_scalar(out=col1[:], in0=col1[:], scalar1=1.0 / D,
                                scalar2=None, op0=OP.mult)
                v.drain()
                v.tensor_scalar(out=tmtile[:], in0=tmtile[:], scalar1=col1[:, 0:1],
                                scalar2=None, op0=OP.subtract)
            v.tensor_tensor(out=tmtile2[:], in0=tmtile[:], in1=tmtile[:], op=OP.mult)
            v.reduce_sum(col2[:], tmtile2[:], axis=AX.X)
            v.drain()
            SV.inc(v.tensor_scalar(out=col2[:], in0=col2[:], scalar1=1.0 / D,
                                   scalar2=EPS, op0=OP.mult, op1=OP.add))

        @blk.scalar
        def _(s):
            SV.wait(s)
            SS.inc(s.activation(out=col3[:], in_=col2[:], func=AF.Sqrt))

        @blk.vector
        def _(v):
            SS.wait(v)
            v.reciprocal(col3[:], col3[:])
            v.drain()
            SV.inc(v.tensor_scalar(out=tmtile[:], in0=tmtile[:], scalar1=col3[:, 0:1],
                                   scalar2=None, op0=OP.mult))

        if out_tm is not None:
            @blk.scalar
            def _(s_):
                SV.wait(s_)
                SGR.inc(s_.dma_start(out=out_tm[ts(i, P), :], in_=tmtile[:]), 16)

        if any(w == "norm" for _, w in outs_fm):
            @blk.vector
            def _(v):
                v.tensor_tensor(out=tmtile2[:], in0=tmtile[:], in1=tmtile[:],
                                op=OP.mult)
                v.reduce_sum(col2[:], tmtile2[:], axis=AX.X)
                v.drain()
                SV.inc(v.tensor_scalar(out=col2[:], in0=col2[:], scalar1=1.0 / D,
                                       scalar2=EPS, op0=OP.mult, op1=OP.add))

            @blk.scalar
            def _(s):
                SV.wait(s)
                SS.inc(s.activation(out=col3[:], in_=col2[:], func=AF.Sqrt))

            @blk.vector
            def _(v):
                SS.wait(v)
                v.reciprocal(col3[:], col3[:])
                v.drain()
                SV.inc(v.tensor_scalar(out=tmtile2[:], in0=tmtile[:],
                                       scalar1=col3[:, 0:1], scalar2=None,
                                       op0=OP.mult))

        for oi, (dst_fm, which) in enumerate(outs_fm):
            buf = tmbf if which == "norm" else tmbf2
            srcten = tmtile2 if which == "norm" else tmtile

            @blk.vector
            def _(v, buf=buf, srcten=srcten):
                ST.wait(v)
                SV.inc(v.tensor_copy(buf[:], srcten[:]))

            for c in range(8):
                sl = c % 2

                @blk.tensor
                def _(t, buf=buf, c=c, sl=sl):
                    SV.wait(t)
                    ST.inc(t.transpose(psTb[:, sl, :], buf[:, ts(c, P)], id_bf[:]))

                @blk.vector
                def _(v, c=c, sl=sl):
                    ST.wait(v)
                    SGBs[sl].wait(v)
                    SV.inc(v.tensor_copy(stg[:, sl, :], psTb[:, sl, :]))

                @blk.gpsimd
                def _(g, dst_fm=dst_fm, c=c, sl=sl, i=i):
                    SV.wait(g)
                    SGBs[sl].inc(g.dma_start(out=dst_fm[ts(c, P), ts(i, P)],
                                             in_=stg[:, sl, :]), 16)

        if extra_tm32 is not None:
            for c in range(8):
                sl = c % 2

                @blk.tensor
                def _(t, c=c, sl=sl):
                    SV.wait(t)
                    ST.inc(t.transpose(psTf[:, sl, :], tmtile[:, ts(c, P)], id_f[:]))

                @blk.vector
                def _(v, c=c, sl=sl):
                    ST.wait(v)
                    SGFs[sl].wait(v)
                    SV.inc(v.tensor_copy(stgf[:, sl, :], psTf[:, sl, :]))

                @blk.gpsimd
                def _(g, c=c, sl=sl, i=i):
                    SV.wait(g)
                    SGFs[sl].inc(g.dma_start(out=extra_tm32[ts(c, P), ts(i, P)],
                                             in_=stgf[:, sl, :]), 16)

    for i in range(NTILE):
        emit_norm_tile(i, xh, do_ln=True,
                       outs_fm=[(xnfm, "pre"), (u0fm, "norm")],
                       out_tm=xntm, extra_tm32=xnfm32)

    with contextlib.ExitStack() as mes:
        msb = lambda n, shp, dt: mes.enter_context(nc.sbuf_tensor(n, shp, dt))
        wbufA = msb("wbufA", [P, 2, 8, P], BF16)
        wbufO = msb("wbufO", [P, 2, NET, P], BF16)
        xptiles = msb("xptiles", [P, NET, 96], BF16)
        dttiles = msb("dttiles", [64, NET, P], BF16)
        urhs = msb("urhs", [P, 8, CH], BF16)
        uhalo = msb("uhalo", [P, 8, 3], BF16)
        xzb = msb("xzb", [P, 515], BF16)
        tails = msb("tails", [P, NET, 3], BF16)
        xbt = msb("xbt", [P, NET, CH], BF16)
        szt = msb("szt", [P, NET, CH], BF16)
        delt = msb("delt", [P, NET, CH], BF16)
        convacc = msb("convacc", [P, CH], F32)
        sps = msb("sps", [P, CH], F32)
        drb = msb("drb", [64, CH], BF16)
        bcsrc = msb("bcsrc", [32, CH], BF16)
        bct = msb("bct", [P, 2, NN, CH], BF16)
        wj = msb("wj", [P, CH], BF16)
        dAf = msb("dAf", [P, NN, SUB], BF16)
        hfld = msb("hfld", [P, NN, SUB], BF16)
        mfld = msb("mfld", [P, NN, SUB], BF16)
        vfld = msb("vfld", [P, NN, SUB], BF16)
        ybuf = msb("ybuf", [P, NSUB, SUB], BF16)
        ybf = msb("ybf", [P, CH], BF16)
        uout = msb("uout", [P, NET, CH], BF16)
        mbst = msb("mbst", [P, 8, CH], BF16)

        def emit_state_cc(l, cc_idx):
            ccs = CCS[cc_idx]
            rs = RS[cc_idx]

            @blk.gpsimd
            def _(g):
                SGA.wait(g)
                with g.If_eq(gpar, 0):
                    g.wait_ge(SST.h, l + 1)
                    g.dma_start(out=ccin[cc_idx][:, 0:256],
                                in_=stateV[:].rearrange("p a b -> p (a b)")
                                ).then_inc(GT.h, 16)
                    g.dma_start(out=ccin[cc_idx][:, 256:512],
                                in_=stateH[:].rearrange("p a b -> p (a b)")
                                ).then_inc(GT.h, 16)
                    GT.v += 32
                    g.wait_ge(GT.h, GT.v)
                g.sem_inc(PK.h, 1)
                PK.v += 1
                ccs.inc(g.collective_compute(
                    "AllReduce", mybir.AluOpType.add, replica_groups=PAIRS,
                    ins=[ccin[cc_idx].opt()], outs=[ccout[cc_idx].opt()]))
                ccs.wait(g)
                rs.inc(g.dma_start(out=recvb[:], in_=ccout[cc_idx]), 16)

        def emit_mamba_layer(l, ufm, cc_idx, epilogue):
            emit_state_cc(l, cc_idx)

            @blk.sync
            def _(e, l=l):
                SGA.wait(e)
                for sg_ in SGBs:
                    sg_.wait(e)
                for sg_ in SGFs:
                    sg_.wait(e)
                if l == 0:
                    ST.wait(e)
                else:
                    ST.wait(e)
                SXP.inc(e.dma_start(out=xptiles[:],
                                   in_=wxp[l].rearrange("(j p) r -> p j r", p=P)), 16)
                SXP.inc(e.dma_start(out=dttiles[:],
                                   in_=wdt[l].rearrange("r (j m) -> r j m", m=P)), 16)

            def emit_state_init(v, l, rs):
                if l == 1:
                    v.wait_ge(PK.h, 1)
                v.memset(stateV[:], 0.0)
                SV.inc(v.memset(stateH[:], 0.0))
                with v.If_eq(vpar, 1):
                    v.wait_ge(rs.h, 16)
                    v.tensor_scalar(out=stateV[:].rearrange("p a b -> p (a b)"),
                                    in0=recvb[:, 0:256], scalar1=maskt[:, 0:1],
                                    scalar2=None, op0=OP.mult)
                    v.tensor_scalar(out=stateH[:].rearrange("p a b -> p (a b)"),
                                    in0=recvb[:, 256:512], scalar1=maskt[:, 0:1],
                                    scalar2=None, op0=OP.mult)
                v.drain()

            def emit_post(k):
                for m in range(8):
                    slot = m % 2

                    @blk.sync
                    def _(e, m=m, slot=slot, l=l):
                        if m >= 2:
                            e.wait_ge(ST.h, ST.v - 1)
                        WOs[slot].inc(e.dma_start(
                            out=wbufO[:, slot, :, :],
                            in_=wout[l][:, ts(m, P)]
                            .rearrange("(kc p) m -> p kc m", p=P)), 16)

                    @blk.tensor
                    def _(t, m=m, slot=slot):
                        WOs[slot].wait(t)
                        waitmk(t, "uout")
                        waitmk(t, "psA" if m % 2 == 0 else "psB")
                        ps = psA if m % 2 == 0 else psB
                        for kc in range(NET):
                            mm = t.matmul(ps[:], wbufO[:, slot, kc, :],
                                          uout[:, kc, :],
                                          start=(kc == 0), stop=(kc == NET - 1))
                        ST.inc(mm)

                    @blk.vector
                    def _(v, m=m):
                        ST.wait(v)
                        ps = psA if m % 2 == 0 else psB
                        SV.inc(v.tensor_copy(mbst[:, m, :], ps[:]))
                    setmk("psA" if m % 2 == 0 else "psB", SV)
                setmk("uoutfree", ST)

                for tau in range(4):
                    rowbase = CH * k + P * tau + 3

                    if epilogue == "r1":
                        @blk.sync
                        def _(e, rowbase=rowbase):
                            SGR.wait(e)
                            SWT.inc(e.dma_start(
                                out=tmtile[:], in_=xntm[rowbase:rowbase + P, :]), 16)

                    for c in range(8):
                        sl = c % 2

                        @blk.tensor
                        def _(t, c=c, tau=tau, sl=sl):
                            SV.wait(t)
                            SS.wait(t)
                            ST.inc(t.transpose(psTb[:, sl, :],
                                               mbst[:, c, ts(tau, P)], id_bf[:]))

                        @blk.vector
                        def _(v, c=c, sl=sl, epilogue=epilogue):
                            ST.wait(v)
                            if epilogue != "r1" and c == 0:
                                v.wait_ge(SGR.h, SGR.v)
                            if epilogue == "r1":
                                SWT.wait(v)
                                SV.inc(v.tensor_tensor(
                                    out=tmtile[:, ts(c, P)],
                                    in0=tmtile[:, ts(c, P)],
                                    in1=psTb[:, sl, :], op=OP.add))
                            else:
                                SV.inc(v.tensor_copy(tmtile[:, ts(c, P)],
                                                     psTb[:, sl, :]))

                    setmk("epiadds", SV)

                    @blk.scalar
                    def _(s_, rowbase=rowbase, epilogue=epilogue):
                        waitmk(s_, "epiadds")
                        dst = r1tm if epilogue == "r1" else mb1tm
                        SGR.inc(s_.dma_start(out=dst[rowbase:rowbase + P, :],
                                             in_=tmtile[:]), 16)


            for k in range(NCH):
                c0 = 3 + CH * k

                @blk.sync
                def _(e, k=k, c0=c0, ufm=ufm):
                    ST.wait(e)
                    SWU.inc(e.dma_start(
                        out=urhs[:],
                        in_=ufm.rearrange("(c p) t -> p c t", p=P)[:, :, c0:c0 + CH]), 16)
                    if k == 0:
                        SWU.inc(e.dma_start(
                            out=uhalo[:],
                            in_=ufm.rearrange("(c p) t -> p c t", p=P)[:, :, 0:3]), 16)

                for m in range(32):
                    slot = m % 2

                    @blk.sync
                    def _(e, m=m, slot=slot, l=l):
                        if m >= 2:
                            e.wait_ge(ST.h, ST.v - 1)
                        WAs[slot].inc(e.dma_start(
                            out=wbufA[:, slot, :, :],
                            in_=winw[l][:, ts(m, P)].rearrange("(kc p) m -> p kc m", p=P)), 16)

                    @blk.tensor
                    def _(t, m=m, slot=slot, k=k, l=l):
                        WAs[slot].wait(t)
                        SWU.wait(t)
                        waitmk(t, "psA" if m % 2 == 0 else "psB")
                        if k == 0 and m < 16:
                            waitmk(t, "psBC")
                        ps = psA if m % 2 == 0 else psB
                        for kc in range(8):
                            mm = t.matmul(ps[:], wbufA[:, slot, kc, :], urhs[:, kc, :],
                                          start=(kc == 0), stop=(kc == 7))
                        ST.inc(mm)
                        if k == 0 and m < 16:
                            for kc in range(8):
                                mm = t.matmul(psBC[:, 3 * m:3 * m + 3],
                                              wbufA[:, slot, kc, :], uhalo[:, kc, :],
                                              start=(kc == 0), stop=(kc == 7))
                            ST.inc(mm)

                    ps = psA if m % 2 == 0 else psB
                    if m < 16:
                        @blk.vector
                        def _(v, m=m, ps=ps, k=k, l=l):
                            ST.wait(v)
                            if k == 0:
                                v.tensor_copy(xzb[:, 0:3], psBC[:, 3 * m:3 * m + 3])
                            else:
                                v.tensor_copy(xzb[:, 0:3], tails[:, m, :])
                            v.tensor_copy(xzb[:, 3:515], ps[:])
                            v.tensor_copy(tails[:, m, :], ps[:, 509:512])
                            v.tensor_scalar(out=convacc[:], in0=xzb[:, 0:CH],
                                            scalar1=cwt[:, l, m, 0:1], scalar2=None,
                                            op0=OP.mult)
                            for kk in range(1, 4):
                                ins_ = v.scalar_tensor_tensor(
                                    out=convacc[:], in0=xzb[:, kk:kk + CH],
                                    scalar=cwt[:, l, m, kk:kk + 1], in1=convacc[:],
                                    op0=OP.mult, op1=OP.add)
                            SV.inc(ins_)

                        setmk("psA" if m % 2 == 0 else "psB", SV)

                        @blk.scalar
                        def _(s, m=m, l=l):
                            SV.wait(s)
                            SS.inc(s.activation(out=xbt[:, m, :], in_=convacc[:],
                                                func=AF.Silu, bias=cbt[:, l, m:m + 1]))
                    else:
                        @blk.scalar
                        def _(s, m=m, ps=ps):
                            ST.wait(s)
                            SS.inc(s.activation(out=szt[:, m - 16, :], in_=ps[:],
                                                func=AF.Silu))
                        setmk("psA" if m % 2 == 0 else "psB", SS)
                    if k == 0 and m == 15:
                        setmk("psBC", SV)

                @blk.tensor
                def _(t):
                    SS.wait(t)
                    waitmk(t, "psB")
                    SXP.wait(t)
                    for j in range(NET):
                        mm = t.matmul(psB[0:96, :], xptiles[:, j, :], xbt[:, j, :],
                                      start=(j == 0), stop=(j == NET - 1))
                    ST.inc(mm)

                @blk.vector
                def _(v):
                    ST.wait(v)
                    v.tensor_copy(drb[:], psB[0:64, :])
                    SV.inc(v.tensor_copy(bcsrc[:], psB[64:96, :]))
                setmk("psB", SV)
                setmk("drb", SV)

                for q in range(8):
                    @blk.tensor
                    def _(t, q=q):
                        waitmk(t, "drb")
                        waitmk(t, "psBC")
                        for u in range(4):
                            n = 4 * q + u
                            mm = t.matmul(psBC[:, ts(u, CH)], selmat[:, n, :],
                                          bcsrc[:, :], start=True, stop=True)
                        ST.inc(mm)

                    @blk.vector
                    def _(v, q=q):
                        ST.wait(v)
                        SV.inc(v.tensor_copy(
                            bct[:, q // 4, 4 * (q % 4):4 * (q % 4) + 4, :]
                            .rearrange("p a b -> p (a b)"),
                            psBC[:]))
                    setmk("psBC", SV)

                for j in range(NET):
                    @blk.tensor
                    def _(t, j=j):
                        waitmk(t, "psA")
                        waitmk(t, "drb")
                        ST.inc(t.matmul(psA[:], dttiles[:, j, :], drb[:],
                                        start=True, stop=True))

                    @blk.vector
                    def _(v, j=j, l=l):
                        ST.wait(v)
                        v.tensor_scalar(out=convacc[:], in0=psA[:],
                                        scalar1=dtbt[:, l, j:j + 1],
                                        scalar2=None, op0=OP.add)
                        SV.inc(v.scalar_tensor_tensor(out=sps[:], in0=convacc[:],
                                                      scalar=-1.0, in1=convacc[:],
                                                      op0=OP.mult, op1=OP.min))
                    setmk("psA", SV)

                    @blk.scalar
                    def _(s, j=j):
                        SV.wait(s)
                        s.activation(out=sps[:], in_=sps[:], func=AF.Exp)
                        s.drain()
                        SS.inc(s.activation(out=sps[:], in_=sps[:], func=AF.Ln,
                                            bias=1.0))

                    @blk.vector
                    def _(v, j=j):
                        SS.wait(v)
                        v.scalar_tensor_tensor(out=convacc[:], in0=convacc[:],
                                               scalar=0.0, in1=sps[:],
                                               op0=OP.max, op1=OP.add)
                        SV.inc(v.tensor_copy(delt[:, j, :], convacc[:]))

                if k > 0:
                    emit_post(k - 1)

                if k == 0:
                    @blk.vector
                    def _(v, l=l, rs=RS[cc_idx]):
                        emit_state_init(v, l, rs)

                for j in range(NET):
                    @blk.vector
                    def _(v, j=j):
                        SS.wait(v)
                        SV.inc(v.tensor_tensor(out=wj[:], in0=delt[:, j, :],
                                               in1=xbt[:, j, :], op=OP.mult))
                    for s_ in range(NSUB):
                        t0 = SUB * s_

                        @blk.vector
                        def _(v, j=j, t0=t0, l=l):
                            v.tensor_tensor(
                                out=mfld[:],
                                in0=wj[:, t0:t0 + SUB].unsqueeze(1)
                                .broadcast_to([P, NN, SUB]),
                                in1=bct[:, 0, :, t0:t0 + SUB], op=OP.mult)
                            for n in range(NN):
                                ins_ = v.tensor_scalar(
                                    out=dAf[:, n, :], in0=delt[:, j, t0:t0 + SUB],
                                    scalar1=neglam[:, l, j, n:n + 1],
                                    scalar2=None, op0=OP.mult)
                            SV.inc(ins_)

                        @blk.scalar
                        def _(s):
                            SV.wait(s)
                            SS.inc(s.activation(
                                out=dAf[:, 0:8, :].rearrange("p a b -> p (a b)"),
                                in_=dAf[:, 0:8, :].rearrange("p a b -> p (a b)"),
                                func=AF.Exp))
                        ssA = SS.v

                        @blk.scalar
                        def _(s):
                            SS.inc(s.activation(
                                out=dAf[:, 8:16, :].rearrange("p a b -> p (a b)"),
                                in_=dAf[:, 8:16, :].rearrange("p a b -> p (a b)"),
                                func=AF.Exp))

                        @blk.vector
                        def _(v, j=j, t0=t0, s_=s_, ssA=ssA):
                            v.wait_ge(SS.h, ssA)
                            for n in range(8):
                                v.tensor_tensor_scan(vfld[:, n, :], betacol[:],
                                                     mfld[:, n, :],
                                                     stateV[:, j, n:n + 1],
                                                     OP.mult, OP.add)
                            for n in range(8):
                                v.tensor_tensor_scan(hfld[:, n, :], dAf[:, n, :],
                                                     vfld[:, n, :],
                                                     stateH[:, j, n:n + 1],
                                                     OP.mult, OP.add)
                            SS.wait(v)
                            for n in range(8, NN):
                                v.tensor_tensor_scan(vfld[:, n, :], betacol[:],
                                                     mfld[:, n, :],
                                                     stateV[:, j, n:n + 1],
                                                     OP.mult, OP.add)
                            for n in range(8, NN):
                                v.tensor_tensor_scan(hfld[:, n, :], dAf[:, n, :],
                                                     vfld[:, n, :],
                                                     stateH[:, j, n:n + 1],
                                                     OP.mult, OP.add)
                            v.tensor_copy(stateV[:, j, :], vfld[:, :, SUB - 1])
                            v.tensor_tensor(out=mfld[:], in0=hfld[:],
                                            in1=bct[:, 1, :, t0:t0 + SUB], op=OP.mult)
                            v.tensor_tensor(out=mfld[:, 0:8, :], in0=mfld[:, 0:8, :],
                                            in1=mfld[:, 8:16, :], op=OP.add)
                            v.tensor_tensor(out=mfld[:, 0:4, :], in0=mfld[:, 0:4, :],
                                            in1=mfld[:, 4:8, :], op=OP.add)
                            v.tensor_tensor(out=mfld[:, 0:2, :], in0=mfld[:, 0:2, :],
                                            in1=mfld[:, 2:4, :], op=OP.add)
                            v.tensor_tensor(out=ybuf[:, s_, :],
                                            in0=mfld[:, 0, :],
                                            in1=mfld[:, 1, :], op=OP.add)
                            SV.inc(v.tensor_copy(stateH[:, j, :],
                                                 hfld[:, :, SUB - 1]))

                    @blk.vector
                    def _(v, j=j, l=l):
                        waitmk(v, "uoutfree")
                        v.scalar_tensor_tensor(
                            out=ybf[:], in0=xbt[:, j, :], scalar=dpt[:, l, j:j + 1],
                            in1=ybuf[:].rearrange("p a b -> p (a b)"),
                            op0=OP.mult, op1=OP.add)
                        SV.inc(v.tensor_tensor(out=uout[:, j, :], in0=ybf[:],
                                               in1=szt[:, j, :], op=OP.mult))

                setmk("uout", SV)

                if k == NCH - 1:
                    @blk.vector
                    def _(v):
                        SST.inc(v.engine_nop())

            emit_post(NCH - 1)

            if epilogue == "r1":
                @blk.gpsimd
                def _(g):
                    with g.If_eq(gpar, 0):
                        g.wait_ge(SGR.h, SGR.v)
                        g.dma_start(out=ccin[1],
                                    in_=r1tm[T:T + 3, :]).then_inc(GT.h, 16)
                        GT.v += 16
                        g.wait_ge(GT.h, GT.v)
                    CCS[1].inc(g.collective_compute(
                        "AllReduce", mybir.AluOpType.add, replica_groups=PAIRS,
                        ins=[ccin[1].opt()], outs=[ccout[1].opt()]))
                    CCS[1].wait(g)
                    with g.If_eq(gpar, 1):
                        g.dma_start(out=r1tm[0:3, :], in_=ccout[1]).then_inc(RS[1].h, 16)
                    with g.If_eq(gpar, 0):
                        g.dma_start(out=r1tm[0:3, 0:512],
                                    in_=zerot[0:3, :]).then_inc(RS[1].h, 16)
                    RS[1].v += 16

        emit_mamba_layer(0, u0fm, 0, "r1")


        for i in range(NTILE):
            emit_norm_tile(i, r1tm, do_ln=False, outs_fm=[(u1fm, "pre")],
                           wait_rs=(RS[1] if i == 0 else None))

        emit_mamba_layer(1, u1fm, 2, "mb1")

    with contextlib.ExitStack() as fes:
        fsb = lambda n, shp, dt: fes.enter_context(nc.sbuf_tensor(n, shp, dt))
        xnrhs = fsb("xnrhs", [P, 8, CH], BF16)
        xnrhs32 = fsb("xnrhs32", [P, 8, P], F32)
        wgbuf = fsb("wgbuf", [P, 2, 8, 4, P], BF16)
        wubuf = fsb("wubuf", [P, 2, 8, 4, P], BF16)
        wdbuf = fsb("wdbuf", [P, 2, 4, 8, P], BF16)
        sgb = fsb("sgb", [P, 4, CH], BF16)
        gtd = fsb("gtd", [P, 4, CH], BF16)
        wbc = fsb("wbc", [P, CH], BF16)
        expwt = fsb("expwt", [P, 8, CH], F32)
        wtsfm = fsb("wtsfm", [17, T], BF16)
        rtile = fsb("rtile", [P, 16], F32)
        rb2 = fsb("rb2", [P, 16], F32)
        rb3 = fsb("rb3", [P, 16], F32)
        rcol = fsb("rcol", [P, 1], F32)
        rcol2 = fsb("rcol2", [P, 1], F32)

        @blk.sync
        def _(e):
            SWX.inc(e.dma_start(out=wtsfm[16:17, :], in_=onesd), 16)

        for i in range(16):
            tok0 = P * i

            @blk.sync
            def _(e, tok0=tok0):
                for sg_ in SGFs:
                    sg_.wait(e)
                ST.wait(e)
                SWX.inc(e.dma_start(
                    out=xnrhs32[:],
                    in_=xnfm32.rearrange("(c p) t -> p c t", p=P)
                    [:, :, 3 + tok0:3 + tok0 + P]), 16)

            @blk.tensor
            def _(t):
                SWX.wait(t)
                t.wait_ge(SV.h, SV.v)
                for kc in range(8):
                    mm = t.matmul(psA[0:16, 0:P], gwt[:, kc, :], xnrhs32[:, kc, :],
                                  start=(kc == 0), stop=(kc == 7))
                ST.inc(mm)

            @blk.vector
            def _(v):
                ST.wait(v)
                SV.inc(v.tensor_copy(stgf[:, 0, :][0:16, :], psA[0:16, 0:P]))

            @blk.tensor
            def _(t):
                SV.wait(t)
                ST.inc(t.transpose(psTf[:, 0, :][:, 0:16], stgf[:, 0, :][0:16, :],
                                   id_f[0:16, 0:16]))

            @blk.vector
            def _(v):
                ST.wait(v)
                SV.inc(v.tensor_tensor(out=rtile[:], in0=psTf[:, 0, 0:16],
                                       in1=gbtile[:], op=OP.add))

            @blk.scalar
            def _(s):
                SV.wait(s)
                SS.inc(s.activation(out=rtile[:], in_=rtile[:], func=AF.Sigmoid))

            @blk.vector
            def _(v):
                SS.wait(v)
                v.reduce_max(rcol[:], rtile[:], axis=AX.X)
                v.drain()
                SV.inc(v.tensor_scalar(out=rb2[:], in0=rtile[:],
                                       scalar1=rcol[:, 0:1], scalar2=None,
                                       op0=OP.subtract))
                v.drain()

            @blk.scalar
            def _(s):
                SV.wait(s)
                SS.inc(s.activation(out=rb2[:], in_=rb2[:], func=AF.Sign, scale=-1.0))
                s.drain()

            @blk.vector
            def _(v):
                SS.wait(v)
                v.tensor_tensor(out=rb3[:], in0=rtile[:], in1=rb2[:], op=OP.mult)
                v.drain()
                v.tensor_scalar(out=rb2[:], in0=rb2[:], scalar1=100.0,
                                scalar2=-100.0, op0=OP.mult, op1=OP.add)
                v.drain()
                v.tensor_tensor(out=rb3[:], in0=rb3[:], in1=rb2[:], op=OP.add)
                v.drain()
                v.reduce_max(rcol2[:], rb3[:], axis=AX.X)
                v.drain()
                SV.inc(v.tensor_scalar(out=rb2[:], in0=rtile[:],
                                       scalar1=rcol2[:, 0:1], scalar2=None,
                                       op0=OP.subtract))
                v.drain()

            @blk.scalar
            def _(s):
                SV.wait(s)
                SS.inc(s.activation(out=rb2[:], in_=rb2[:], func=AF.Sign))
                s.drain()

            @blk.vector
            def _(v):
                SS.wait(v)
                v.tensor_scalar(out=rb2[:], in0=rb2[:], scalar1=1.0, scalar2=1.0,
                                op0=OP.add, op1=OP.min)
                v.drain()
                v.tensor_tensor(out=rb3[:], in0=rtile[:], in1=rb2[:], op=OP.mult)
                v.drain()
                v.tensor_tensor(out=rcol[:], in0=rcol[:], in1=rcol2[:], op=OP.add)
                v.drain()
                v.reciprocal(rcol[:], rcol[:])
                v.drain()
                SV.inc(v.tensor_scalar(out=rb3[:], in0=rb3[:], scalar1=rcol[:, 0:1],
                                       scalar2=None, op0=OP.mult))
                v.drain()

            @blk.tensor
            def _(t):
                SV.wait(t)
                ST.inc(t.transpose(psTf[:, 0, :][0:16, :], rb3[:], id_f[:]))

            @blk.vector
            def _(v, i=i):
                ST.wait(v)
                SV.inc(v.tensor_copy(wtsfm[0:16, ts(i, P)], psTf[0:16, 0, :]))

        for k in range(NCH):
            c0 = 3 + CH * k

            @blk.sync
            def _(e, c0=c0):
                ST.wait(e)
                for sg_ in SGBs:
                    sg_.wait(e)
                SWX.inc(e.dma_start(
                    out=xnrhs[:],
                    in_=xnfm.rearrange("(c p) t -> p c t", p=P)[:, :, c0:c0 + CH]), 16)

            @blk.vector
            def _(v):
                SWX.wait(v)
                SV.inc(v.memset(expwt[:], 0.0))

            for e_ in range(17):
                slot = e_ % 2

                @blk.sync
                def _(e, e_=e_, slot=slot, k=k):
                    if e_ >= 2 or k > 0:
                        e.wait_ge(ST.h, ST.v - 2)
                    srcg = wewg[e_] if e_ < 16 else wswg
                    srcu = wewu[e_] if e_ < 16 else wswu
                    srcd = wewd[e_] if e_ < 16 else wswd
                    WFs[slot].inc(e.dma_start(
                        out=wgbuf[:, slot, :, :, :],
                        in_=srcg.rearrange("(kc p) (mc m) -> p kc mc m", p=P, m=P)), 16)
                    WFs[slot].inc(e.dma_start(
                        out=wubuf[:, slot, :, :, :],
                        in_=srcu.rearrange("(kc p) (mc m) -> p kc mc m", p=P, m=P)), 16)
                    WFs[slot].inc(e.dma_start(
                        out=wdbuf[:, slot, :, :, :],
                        in_=srcd.rearrange("(kc p) (mc m) -> p kc mc m", p=P, m=P)), 16)

                @blk.tensor
                def _(t, e_=e_, k=k):
                    SV.wait(t)
                    SWX.wait(t)
                    t.wait_ge(SS.h, SS.v)
                    ST.inc(t.matmul(psBC[:, 0:CH], selmat[0:17, e_, :],
                                    wtsfm[:, ts(k, CH)],
                                    start=True, stop=True))

                @blk.vector
                def _(v):
                    ST.wait(v)
                    SV.inc(v.tensor_copy(wbc[:], psBC[:, 0:CH]))

                for mi in range(4):
                    @blk.tensor
                    def _(t, e_=e_, slot=slot, mi=mi):
                        WFs[slot].wait(t)
                        SWX.wait(t)
                        t.wait_ge(SS.h, SS.v)
                        t.wait_ge(SV.h, SV.v)
                        for kc in range(8):
                            mm = t.matmul(psA[:], wgbuf[:, slot, kc, mi, :],
                                          xnrhs[:, kc, :],
                                          start=(kc == 0), stop=(kc == 7))
                        ST.inc(mm)

                    @blk.scalar
                    def _(s, mi=mi):
                        ST.wait(s)
                        SS.inc(s.activation(out=sgb[:, mi, :], in_=psA[:],
                                            func=AF.Silu))

                    @blk.tensor
                    def _(t, e_=e_, slot=slot, mi=mi):
                        t.wait_ge(SS.h, SS.v)
                        t.wait_ge(SV.h, SV.v)
                        for kc in range(8):
                            mm = t.matmul(psB[:], wubuf[:, slot, kc, mi, :],
                                          xnrhs[:, kc, :],
                                          start=(kc == 0), stop=(kc == 7))
                        ST.inc(mm)

                    @blk.vector
                    def _(v, mi=mi):
                        ST.wait(v)
                        SS.wait(v)
                        v.tensor_tensor(out=gtd[:, mi, :], in0=sgb[:, mi, :],
                                        in1=psB[:], op=OP.mult)
                        SV.inc(v.tensor_tensor(out=gtd[:, mi, :], in0=gtd[:, mi, :],
                                               in1=wbc[:], op=OP.mult))


                for md in range(8):
                    @blk.tensor
                    def _(t, e_=e_, slot=slot, md=md):
                        t.wait_ge(SV.h, SV.v)
                        t.wait_ge(SS.h, SS.v)
                        ps = psA if md % 2 == 0 else psB
                        for kc in range(4):
                            mm = t.matmul(ps[:], wdbuf[:, slot, kc, md, :],
                                          gtd[:, kc, :],
                                          start=(kc == 0), stop=(kc == 3))
                        ST.inc(mm)

                    @blk.vector
                    def _(v, md=md):
                        ST.wait(v)
                        ps = psA if md % 2 == 0 else psB
                        SV.inc(v.tensor_tensor(out=expwt[:, md, :],
                                               in0=expwt[:, md, :],
                                               in1=ps[:], op=OP.add))

            for tau in range(4):
                tok0 = CH * k + P * tau

                @blk.sync
                def _(e, tok0=tok0, k=k, tau=tau):
                    SGR.wait(e)
                    SGO.wait(e)
                    SWT.inc(e.dma_start(out=tmtile[:],
                                        in_=xh[3 + tok0:3 + tok0 + P, :]), 16)
                    SWT.inc(e.dma_start(out=tmtile2[:],
                                        in_=xntm[3 + tok0:3 + tok0 + P, :]), 16)

                @blk.vector
                def _(v):
                    SWT.wait(v)
                    SV.inc(v.tensor_tensor(out=tmtile[:], in0=tmtile[:],
                                           in1=tmtile2[:], op=OP.add))

                @blk.sync
                def _(e, tok0=tok0):
                    SV.wait(e)
                    SWT.inc(e.dma_start(out=tmtile2[:],
                                        in_=r1tm[3 + tok0:3 + tok0 + P, :]), 16)

                @blk.vector
                def _(v):
                    SWT.wait(v)
                    SV.inc(v.tensor_tensor(out=tmtile[:], in0=tmtile[:],
                                           in1=tmtile2[:], op=OP.add))

                @blk.sync
                def _(e, tok0=tok0):
                    SV.wait(e)
                    SWT.inc(e.dma_start(out=tmtile2[:],
                                        in_=mb1tm[3 + tok0:3 + tok0 + P, :]), 16)

                @blk.vector
                def _(v):
                    SWT.wait(v)
                    SV.inc(v.tensor_tensor(out=tmtile[:], in0=tmtile[:],
                                           in1=tmtile2[:], op=OP.add))

                for c in range(8):
                    sl = c % 2

                    @blk.tensor
                    def _(t, c=c, tau=tau, sl=sl):
                        SV.wait(t)
                        SS.wait(t)
                        ST.inc(t.transpose(psTf[:, sl, :], expwt[:, c, ts(tau, P)],
                                           id_f[:]))

                    @blk.vector
                    def _(v, c=c, sl=sl):
                        ST.wait(v)
                        SV.inc(v.tensor_tensor(out=tmtile[:, ts(c, P)],
                                               in0=tmtile[:, ts(c, P)],
                                               in1=psTf[:, sl, :], op=OP.add))

                @blk.gpsimd
                def _(g, tok0=tok0):
                    SV.wait(g)
                    SGO.inc(g.dma_start(out=out[tok0:tok0 + P, :], in_=tmtile[:]), 16)

    @blk.gpsimd
    def _(g):
        SGO.wait(g)

    es.close()
    return nc


_CACHE = {}


def kernel(L0_attn_x, L0_attn_norm_w, L0_attn_in_proj_w, L0_attn_conv_w,
           L0_attn_conv_b, L0_attn_x_proj_w, L0_attn_dt_w, L0_attn_dt_b,
           L0_attn_A_log, L0_attn_Dp, L0_attn_out_w, L0_ffn_gate_w,
           L0_ffn_gate_b, L0_ffn_shared_wg, L0_ffn_shared_wu, L0_ffn_shared_wd,
           L0_ffn_experts_wg, L0_ffn_experts_wu, L0_ffn_experts_wd):
    f32 = lambda a: np.ascontiguousarray(np.asarray(a, dtype=np.float32))
    x = f32(L0_attn_x)
    B, L, D_ = x.shape

    common = {
        "inwT": f32(np.transpose(f32(L0_attn_in_proj_w), (0, 2, 1))),
        "xpwT": f32(np.transpose(f32(L0_attn_x_proj_w), (0, 2, 1))),
        "dtwT": f32(np.transpose(f32(L0_attn_dt_w), (0, 2, 1))),
        "outwT": f32(np.transpose(f32(L0_attn_out_w), (0, 2, 1))),
        "cwp": f32(np.transpose(np.reshape(f32(L0_attn_conv_w), (2, NET, P, 4)), (2, 0, 1, 3))),
        "cbp": f32(np.transpose(np.reshape(f32(L0_attn_conv_b), (2, NET, P)), (2, 0, 1))),
        "dtbp": f32(np.transpose(np.reshape(f32(L0_attn_dt_b), (2, NET, P)), (2, 0, 1))),
        "dpp": f32(np.transpose(np.reshape(f32(L0_attn_Dp), (2, NET, P)), (2, 0, 1))),
        "alogp": f32(np.transpose(np.reshape(f32(L0_attn_A_log), (2, NET, P, NN)), (2, 0, 1, 3))),
        "nwp": f32(np.transpose(np.reshape(f32(L0_attn_norm_w), (2, 8, P)), (2, 0, 1))),
        "gwp": f32(np.reshape(f32(L0_ffn_gate_w), (8, P, 16)).transpose(1, 0, 2)),
        "gbt": f32(np.broadcast_to(f32(L0_ffn_gate_b), (P, 16))),
        "maskc": None,
        "idbf": np.eye(P, dtype=BFnp),
        "idf": np.eye(P, dtype=np.float32),
        "selp": np.ascontiguousarray(
            np.transpose(np.eye(32, dtype=BFnp)[:, :, None]
                         * np.ones((1, 1, P), dtype=BFnp), (0, 1, 2))),
        "onesd": np.ones((1, T), dtype=BFnp),
        "swg": f32(L0_ffn_shared_wg), "swu": f32(L0_ffn_shared_wu),
        "swd": f32(L0_ffn_shared_wd),
        "ewg": f32(L0_ffn_experts_wg), "ewu": f32(L0_ffn_experts_wu),
        "ewd": f32(L0_ffn_experts_wd),
    }

    in_maps = []
    for c in range(8):
        b = c // 2
        odd = c % 2
        start = T * odd
        xhc = np.zeros((TH, D_), np.float32)
        if odd:
            xhc[0:3] = x[b, start - 3:start]
        xhc[3:3 + T] = x[b, start:start + T]
        m = dict(common)
        m["xh"] = xhc
        m["maskc"] = np.full((P, 1), float(odd), np.float32)
        in_maps.append(m)

    if "nc" not in _CACHE:
        _CACHE["nc"] = build()
    nc = _CACHE["nc"]
    res = run_bass_kernel_spmd(nc, in_maps, core_ids=list(range(8)))
    outp = np.empty((B, L, D_), np.float32)
    for c in range(8):
        b = c // 2
        odd = c % 2
        outp[b, T * odd:T * odd + T] = np.asarray(res.results[c]["out"])
    return outp
